# version 01
# baseline (speedup 1.0000x reference)
import numpy as np
import ml_dtypes
from contextlib import ExitStack

import concourse.bass as bass
import concourse.mybir as mybir
from concourse.bass_utils import run_bass_kernel_spmd

F32 = mybir.dt.float32
BF16 = mybir.dt.bfloat16
I32 = mybir.dt.int32
AF = mybir.ActivationFunctionType
ALU = mybir.AluOpType
AXX = mybir.AxisListType.X
BF = ml_dtypes.bfloat16

NC = 8
TL = 1024
NT = 8
D = 2048
DC = 16
LAYERS = (0, 1, 2, 3)
FUSED = False
DEBUG = []

BPC = {
    "l1_ln_g": 0, "l1_ln_b": 16, "l1_bq": 32, "l1_bk": 48, "l1_bv": 64,
    "l1_bo": 80, "l1_b2": 96, "l2_bq": 112, "l2_bk": 128, "l2_bv": 144,
    "l2_bo": 160, "l2_bias": 176, "l1_b1": 192,
}
BP_COLS = 256

L3_CHUNKLIST = [list(range(max(0, 4 * t - 4), min(15, 4 * t + 8) + 1))
                for t in range(4)]
L3_NCHUNK = sum(len(cl) for cl in L3_CHUNKLIST)


class Builder:
    ENGINES = ("sync", "gpsimd", "vector", "scalar", "tensor")

    def __init__(self, nc, es):
        self.nc = nc
        self.es = es
        self.q = {e: [] for e in self.ENGINES}
        self.sems = {}
        self.cnt = {}
        self.waited = {e: {} for e in self.ENGINES}

    def sem(self, name):
        if name not in self.sems:
            self.sems[name] = self.es.enter_context(self.nc.semaphore(name))
            self.cnt[name] = 0
        return self.sems[name]

    def tok(self, name):
        self.sem(name)
        return (name, self.cnt[name])

    def op(self, eng, build, waits=(), inc=None, n=1):
        tokn = None
        if inc is not None:
            self.sem(inc)
            self.cnt[inc] += n
            tokn = (inc, self.cnt[inc])
        wl = []
        for t in waits:
            if t is None:
                continue
            s, v = t
            if v <= 0 or self.waited[eng].get(s, 0) >= v:
                continue
            self.waited[eng][s] = v
            self.sem(s)
            wl.append((self.sems[s], v))

        def f(e, wl=wl, build=build, tokn=tokn, amt=n):
            for s, v in wl:
                e.wait_ge(s, v)
            ins = build(e)
            if tokn is not None:
                ins.then_inc(self.sems[tokn[0]], amt)
        self.q[eng].append(f)
        return tokn

    def wait(self, eng, *toks):
        self.op(eng, lambda e: None, waits=[t for t in toks if t])

    def emit(self):
        with self.nc.Block() as blk:
            for e in self.ENGINES:
                fns = self.q[e]
                getattr(blk, e)(lambda eng, fns=fns: [f(eng) for f in fns])


def tmax(*toks):
    best = {}
    for t in toks:
        if t is None:
            continue
        s, v = t
        if best.get(s, -1) < v:
            best[s] = v
    return [(s, v) for s, v in best.items()]


def _fm(a2d):
    T = a2d.shape[0]
    return np.ascontiguousarray(a2d.reshape(T, DC, 128).transpose(1, 2, 0))


def _build_biaspack(ins):
    bp = np.zeros((128, BP_COLS), np.float32)

    def put(key, name):
        if name not in ins:
            return
        c0 = BPC[key]
        v = np.asarray(ins[name], np.float32)
        n = v.size // 128
        bp[:, c0:c0 + n] = v.reshape(n, 128).T
    put("l1_ln_g", "L1_attn_ln_g"); put("l1_ln_b", "L1_attn_ln_b")
    put("l1_bq", "L1_attn_bq"); put("l1_bk", "L1_attn_bk")
    put("l1_bv", "L1_attn_bv"); put("l1_bo", "L1_attn_bo")
    put("l1_b2", "L1_ffn_b2"); put("l1_b1", "L1_ffn_b1")
    put("l2_bq", "L2_attn_bq"); put("l2_bk", "L2_attn_bk")
    put("l2_bv", "L2_attn_bv"); put("l2_bo", "L2_attn_bo")
    put("l2_bias", "L2_ffn_bias")
    return bp


def _build_masks_l1(p):
    m = np.zeros((16, 128, 512), np.float32)
    kr = np.arange(128)[:, None]
    qr = np.arange(128)[None, :]
    for idx in range(16):
        tb = idx // 8
        kk = idx % 8 + 8 * tb
        for jj in range(4):
            gq = 2 * (4 * tb + jj) + p
            if kk < gq:
                blk = np.ones((128, 128), np.float32)
            elif kk == gq:
                blk = (kr <= qr).astype(np.float32)
            else:
                blk = np.zeros((128, 128), np.float32)
            m[idx, :, jj * 128:(jj + 1) * 128] = blk
    return m.astype(BF)


def _build_masks_l3(p):
    out = np.zeros((L3_NCHUNK, 128, 256), np.float32)
    kr = np.arange(128)[:, None]
    e = 0
    for tb in range(4):
        for kk in L3_CHUNKLIST[tb]:
            for jj in range(2):
                gq = 2 * (2 * tb + jj) + p
                qa = 128 * gq + np.arange(128)[None, :]
                ka = 128 * kk + kr
                blk = ((ka <= qa) & (qa - ka <= 512)).astype(np.float32)
                out[e, :, jj * 128:(jj + 1) * 128] = blk
            e += 1
    return out.astype(BF)


def _build_rope(p):
    tiles = [2 * j + p for j in range(NT)]
    pos = np.concatenate([128 * t + np.arange(128) for t in tiles]).astype(np.float64)
    inv = 1.0 / (10000.0 ** (np.arange(64, dtype=np.float64) / 64.0))
    fr = np.outer(inv, pos)
    cc, ss = np.cos(fr), np.sin(fr)
    cos = np.concatenate([cc, cc], axis=0).astype(np.float32)
    sin = np.concatenate([ss, ss], axis=0).astype(np.float32)
    return cos.astype(BF), sin.astype(BF)


def make_in_maps(ins, x_fm_per_core, layers):
    ident_f = np.eye(128, dtype=np.float32)
    ident_b = ident_f.astype(BF)
    bp = _build_biaspack(ins)
    maps = []
    for core in range(NC):
        p = core % 2
        m = {"x_fm": x_fm_per_core[core], "identf": ident_f,
             "identb": ident_b, "biaspack": bp}
        if 0 in layers:
            for k in ("L0_attn_w_qkv", "L0_attn_w_out", "L0_ffn_w_gate",
                      "L0_ffn_w_up", "L0_ffn_w_down"):
                m[k] = ins[k]
        if 1 in layers:
            for k in ("L1_attn_Wq", "L1_attn_Wk", "L1_attn_Wv", "L1_attn_Wo"):
                m[k] = ins[k]
            m["b_w1s"] = np.ascontiguousarray(ins["L1_ffn_w1"][core * 1024:(core + 1) * 1024])
            m["b_w2s"] = np.ascontiguousarray(ins["L1_ffn_w2"][core * 256:(core + 1) * 256])
            m["m1"] = _build_masks_l1(p)
        if 2 in layers:
            for k in ("L2_attn_Wq", "L2_attn_Wk", "L2_attn_Wv", "L2_attn_Wo"):
                m[k] = ins[k]
            m["b_l2w"] = np.ascontiguousarray(ins["L2_ffn_weight"][core * 256:(core + 1) * 256])
        if 3 in layers:
            for k in ("L3_attn_wq", "L3_attn_wk", "L3_attn_wv", "L3_attn_wo"):
                m[k] = ins[k]
            m["lut3"] = np.ascontiguousarray(np.tile(
                np.asarray(ins["L3_ffn_lut"], np.float32).reshape(1, 16),
                (128, 1)))
            m["b_gidx"] = np.ascontiguousarray(ins["L3_ffn_gate_idx"][core * 688:(core + 1) * 688])
            m["b_uidx"] = np.ascontiguousarray(ins["L3_ffn_up_idx"][core * 688:(core + 1) * 688])
            m["b_didx"] = np.ascontiguousarray(ins["L3_ffn_down_idx"][core * 256:(core + 1) * 256])
            m["m3"] = _build_masks_l3(p)
            m["cos3"], m["sin3"] = _build_rope(p)
            R = np.zeros((128, 128), np.float32)
            for mm_ in range(64):
                R[mm_ + 64, mm_] = -1.0
                R[mm_, mm_ + 64] = 1.0
            m["rotmat"] = R.astype(BF)
        maps.append(m)
    return maps


class Ctx:
    pass


def build_program(layers):
    nc = bass.Bass("TRN2", target_bir_lowering=False)
    es = ExitStack()
    es.__enter__()
    c = Ctx()
    c.nc = nc
    c.layers = layers
    dt = nc.dram_tensor

    c.x_in = dt("x_fm", [DC, 128, TL], F32, kind="ExternalInput")
    c.x_out = dt("x_out", [DC, 128, TL], F32, kind="ExternalOutput")
    c.identf_d = dt("identf", [128, 128], F32, kind="ExternalInput")
    c.identb_d = dt("identb", [128, 128], BF16, kind="ExternalInput")
    c.bp_d = dt("biaspack", [128, BP_COLS], F32, kind="ExternalInput")
    c.dbg = {}
    for name in DEBUG:
        dtp = F32 if name.endswith("32") else BF16
        c.dbg[name] = dt("dbg_" + name, [DC, 128, TL], dtp, kind="ExternalOutput")

    if 0 in layers:
        c.w_qkv = dt("L0_attn_w_qkv", [6144, 2048], F32, kind="ExternalInput")
        c.w_out0 = dt("L0_attn_w_out", [2048, 2048], F32, kind="ExternalInput")
        c.w_g0 = dt("L0_ffn_w_gate", [8192, 2048], F32, kind="ExternalInput")
        c.w_u0 = dt("L0_ffn_w_up", [8192, 2048], F32, kind="ExternalInput")
        c.w_d0 = dt("L0_ffn_w_down", [2048, 8192], F32, kind="ExternalInput")
        c.sQKV = dt("sQKV", [6144, 2048], BF16)
        c.sWO0 = dt("sWO0", [2048, 2048], BF16)
        c.sWG0 = dt("sWG0", [8192, 2048], BF16)
        c.sWU0 = dt("sWU0", [8192, 2048], BF16)
        c.sWD0 = dt("sWD0", [2048, 8192], BF16)
    if 1 in layers:
        c.w1q = dt("L1_attn_Wq", [2048, 2048], F32, kind="ExternalInput")
        c.w1k = dt("L1_attn_Wk", [2048, 2048], F32, kind="ExternalInput")
        c.w1v = dt("L1_attn_Wv", [2048, 2048], F32, kind="ExternalInput")
        c.w1o = dt("L1_attn_Wo", [2048, 2048], F32, kind="ExternalInput")
        c.b_w1s = dt("b_w1s", [1024, 2048], F32, kind="ExternalInput")
        c.b_w2s = dt("b_w2s", [256, 8192], F32, kind="ExternalInput")
        c.m1_d = dt("m1", [16, 128, 512], BF16, kind="ExternalInput")
        c.s1q = dt("s1q", [2048, 2048], BF16)
        c.s1k = dt("s1k", [2048, 2048], BF16)
        c.s1v = dt("s1v", [2048, 2048], BF16)
        c.s1o = dt("s1o", [2048, 2048], BF16)
        c.qw1l = dt("qw1l", [1024, 2048], BF16)
        c.qw2l = dt("qw2l", [256, 8192], BF16)
        c.qw1g = dt("qw1g", [8192, 2048], BF16)
        c.qw2g = dt("qw2g", [2048, 8192], BF16)
        c.scl1l = dt("scl1l", [1280, 1], F32)
        c.scl1g = dt("scl1g", [10240, 1], F32)
        c.h1 = dt("h1", [64, 128, TL], F32)
        c.ar1i = dt("ar1i", [1, 1], F32)
        c.ar1o = dt("ar1o", [1, 1], F32)
        c.ar2i = dt("ar2i", [1, 1], F32)
        c.ar2o = dt("ar2o", [1, 1], F32)
    if 2 in layers:
        c.w2q = dt("L2_attn_Wq", [2048, 2048], F32, kind="ExternalInput")
        c.w2k = dt("L2_attn_Wk", [2048, 2048], F32, kind="ExternalInput")
        c.w2v = dt("L2_attn_Wv", [2048, 2048], F32, kind="ExternalInput")
        c.w2o = dt("L2_attn_Wo", [2048, 2048], F32, kind="ExternalInput")
        c.b_l2w = dt("b_l2w", [256, 2048], F32, kind="ExternalInput")
        c.s2q = dt("s2q", [2048, 2048], BF16)
        c.s2k = dt("s2k", [2048, 2048], BF16)
        c.s2v = dt("s2v", [2048, 2048], BF16)
        c.s2o = dt("s2o", [2048, 2048], BF16)
        c.bwl = dt("bwl", [256, 2048], BF16)
        c.bwg = dt("bwg", [2048, 2048], BF16)
    if 3 in layers:
        c.w3q = dt("L3_attn_wq", [2048, 2048], F32, kind="ExternalInput")
        c.w3k = dt("L3_attn_wk", [512, 2048], F32, kind="ExternalInput")
        c.w3v = dt("L3_attn_wv", [512, 2048], F32, kind="ExternalInput")
        c.w3o = dt("L3_attn_wo", [2048, 2048], F32, kind="ExternalInput")
        c.lut_d = dt("lut3", [128, 16], F32, kind="ExternalInput")
        c.b_gidx = dt("b_gidx", [688, 2048], I32, kind="ExternalInput")
        c.b_uidx = dt("b_uidx", [688, 2048], I32, kind="ExternalInput")
        c.b_didx = dt("b_didx", [256, 5504], I32, kind="ExternalInput")
        c.m3_d = dt("m3", [L3_NCHUNK, 128, 256], BF16, kind="ExternalInput")
        c.cos_d = dt("cos3", [128, TL], BF16, kind="ExternalInput")
        c.sin_d = dt("sin3", [128, TL], BF16, kind="ExternalInput")
        c.rot_d = dt("rotmat", [128, 128], BF16, kind="ExternalInput")
        c.k3l = dt("k3l", [512, TL], BF16)
        c.k3g = dt("k3g", [1024, TL], BF16)
        c.v3l = dt("v3l", [TL, 512], BF16)
        c.v3g = dt("v3g", [2 * TL, 512], BF16)
        c.s3q = dt("s3q", [2048, 2048], BF16)
        c.s3k = dt("s3k", [512, 2048], BF16)
        c.s3v = dt("s3v", [512, 2048], BF16)
        c.s3o = dt("s3o", [2048, 2048], BF16)
        c.gwl = dt("gwl", [688, 2048], BF16)
        c.uwl = dt("uwl", [688, 2048], BF16)
        c.dwl = dt("dwl", [256, 5504], BF16)
        c.gwg = dt("gwg", [5504, 2048], BF16)
        c.uwg = dt("uwg", [5504, 2048], BF16)
        c.dwg = dt("dwg", [2048, 5504], BF16)

    c.qb = dt("qb", [16, 128, TL], BF16)
    c.kl = dt("kl", [2048, TL], BF16)
    c.kg = dt("kg", [4096, TL], BF16)
    c.vl = dt("vl", [NT * 128, 2080], BF16)
    c.vg = dt("vg", [2 * NT * 128, 2080], BF16)

    b = Builder(nc, es)
    c.b = b

    def sbt(name, shape, dtp):
        return es.enter_context(nc.sbuf_tensor(name, shape, dtp))

    c.x_sb = sbt("x_sb", [128, DC * TL], F32)
    c.a_a = sbt("a_a", [128, DC * TL], BF16)
    c.a_b = sbt("a_b", [128, DC * TL], BF16)
    c.a_h = sbt("a_h", [128, DC * TL], BF16)
    c.pt = sbt("pt", [128, 2048], BF16)
    c.ptT = sbt("ptT", [128, 512], BF16)
    c.wk = sbt("wk", [128, 4096], BF16)
    c.kh = c.wk[:, :]
    c.wring = c.wk[:, 0:3072]
    c.vh = sbt("vh", [128, 2 * 2080], BF16)
    c.qh = sbt("qh", [128, 1024], BF16)
    c.pp = sbt("pp", [128, 4 * 512], BF16)
    c.mk = sbt("mk", [128, 2 * 512], BF16)
    c.f32t = sbt("f32t", [128, 1024], F32)
    c.rows = sbt("rows", [1, 2304], F32)
    c.bp = sbt("bp", [128, BP_COLS], F32)
    c.identf = sbt("identf_s", [128, 128], F32)
    c.identb = sbt("identb_s", [128, 128], BF16)
    c.ones_c = sbt("ones_c", [128, 2], BF16)
    c.ones_rf = sbt("ones_rf", [1, 128], F32)
    c.sclr = sbt("sclr", [128, 96], F32)
    c.stat = sbt("stat", [128, 16], F32)
    c.stg = c.pp
    if 3 in layers:
        c.cosb = c.pp[:, 0:TL]
        c.sinb = c.pp[:, TL:2 * TL]
        c.lutb = sbt("lutb", [128, 16], F32)
        c.rotb = sbt("rotb", [128, 128], BF16)
    c.sbuf_slack = nc.sbuf_bytes_remaining

    c.psA = es.enter_context(nc.psum_tensor("psA", [128, 2048], F32))
    c.psB = es.enter_context(nc.psum_tensor("psB", [128, 2048], F32))
    c.ps = [c.psA, c.psB]

    c.combine_toks = {}
    c.combine_last = None
    c.arena_free = [None, None]
    c.wring_free = [None] * 6

    emit_prelude(c)
    for li in layers:
        [emit_layer0, emit_layer1, emit_layer2, emit_layer3][li](c)
    emit_epilogue(c)
    b.emit()
    es.__exit__(None, None, None)
    return nc, c


def emit_prelude(c):
    b = c.b
    b.op("sync", lambda e: e.dma_start(out=c.identf[:, :], in_=c.identf_d[:, :]),
         inc="wl", n=16)
    b.op("sync", lambda e: e.dma_start(out=c.identb[:, :], in_=c.identb_d[:, :]),
         inc="wl", n=16)
    b.op("sync", lambda e: e.dma_start(out=c.bp[:, :], in_=c.bp_d[:, :]),
         inc="wl", n=16)
    for ch in range(DC):
        b.op("sync", lambda e, ch=ch: e.dma_start(
            out=c.x_sb[:, ch * TL:(ch + 1) * TL], in_=c.x_in[ch, :, :]),
            inc="wl", n=16)
    c.x_tok = b.tok("wl")
    c.const_tok = c.x_tok
    b.op("vector", lambda e: e.memset(c.ones_c[:, :], 1.0), inc="dv")
    b.op("vector", lambda e: e.memset(c.ones_rf[:, :], 1.0), inc="dv")
    b.op("vector", lambda e: e.memset(c.sclr[:, 0:1], 1e-5), inc="dv")
    c.ones_tok = b.tok("dv")


def emit_epilogue(c):
    b = c.b
    for ch in range(DC):
        b.op("sync", lambda e, ch=ch: e.dma_start(
            out=c.x_out[ch, :, :], in_=c.x_sb[:, ch * TL:(ch + 1) * TL]),
            waits=[c.combine_toks.get(ch)], inc="st", n=16)
    b.wait("sync", b.tok("st"))


def dbg_dump(c, name, src_sb, bf16=True, nchunks=DC, waits=()):
    if name not in c.dbg:
        return
    b = c.b
    for ch in range(nchunks):
        b.op("sync", lambda e, ch=ch: e.dma_start(
            out=c.dbg[name][ch, :, :], in_=src_sb[:, ch * TL:(ch + 1) * TL]),
            waits=list(waits), inc="st", n=16)


def cast_weights(c, src, dst, tok_name="cw"):
    b = c.b
    rows, cols = src.shape
    step = max(128, (4 << 20) // (cols * 4) // 128 * 128)
    out = []
    r = 0
    while r < rows:
        rr = min(step, rows - r)
        t = b.op("gpsimd", lambda e, r=r, rr=rr, src=src, dst=dst: e.dma_start(
            out=dst[r:r + rr, :], in_=src[r:r + rr, :]), inc=tok_name, n=16)
        out.append((r + rr, t))
        r += rr
    return out


def cast_tok(cast_list, row_end=1 << 30):
    for re_, t in cast_list:
        if row_end <= re_:
            return t
    return cast_list[-1][1]


def ln(c, quant_max=False):
    b = c.b
    psA, psB = c.psA, c.psB
    pa_free = c.arena_free[0]
    pb_free = c.arena_free[1]
    xb_toks = []
    for ch in range(DC):
        t = b.op("vector", lambda e, ch=ch: e.tensor_copy(
            out=c.a_h[:, ch * TL:(ch + 1) * TL],
            in_=c.x_sb[:, ch * TL:(ch + 1) * TL]),
            waits=[c.x_tok, c.combine_last, getattr(c, "pt_free", None)],
            inc="dv")
        xb_toks.append(t)
    sq_mm = [None, None]
    for ch in range(DC):
        half0 = (ch % 2) * 1024
        t = b.op("vector", lambda e, ch=ch, half0=half0: e.tensor_tensor(
            out=c.pt[:, half0:half0 + TL],
            in0=c.a_h[:, ch * TL:(ch + 1) * TL],
            in1=c.a_h[:, ch * TL:(ch + 1) * TL], op=ALU.mult),
            waits=[xb_toks[ch], sq_mm[ch % 2], getattr(c, "pt_free", None)],
            inc="dv")
        for hf in range(2):
            b.op("tensor", lambda e, ch=ch, hf=hf: e.matmul(
                psA[0:1, hf * 512:(hf + 1) * 512], c.ones_c[:, 0:1],
                c.a_h[:, ch * TL + hf * 512:ch * TL + (hf + 1) * 512],
                start=(ch == 0), stop=(ch == DC - 1)),
                waits=[xb_toks[ch], c.ones_tok, c.combine_last, pa_free],
                inc="pe")
            t2 = b.op("tensor", lambda e, ch=ch, hf=hf, half0=half0: e.matmul(
                psA[0:1, 1024 + hf * 512:1024 + (hf + 1) * 512], c.ones_c[:, 0:1],
                c.pt[:, half0 + hf * 512:half0 + (hf + 1) * 512],
                start=(ch == 0), stop=(ch == DC - 1)),
                waits=[t], inc="pe")
        sq_mm[ch % 2] = t2
    mm_done = b.tok("pe")
    c.pt_free = None
    c.ln_ah_free = mm_done
    b.op("vector", lambda e: e.tensor_scalar(
        out=c.rows[0:1, 0:1024], in0=psA[0:1, 0:1024], scalar1=1.0 / 2048.0,
        scalar2=None, op0=ALU.mult), waits=[mm_done], inc="dv")
    b.op("vector", lambda e: e.tensor_scalar(
        out=c.f32t[0:1, 0:1024], in0=psA[0:1, 1024:2048], scalar1=1.0 / 2048.0,
        scalar2=None, op0=ALU.mult), inc="dv")
    b.op("vector", lambda e: e.tensor_tensor(
        out=psA[0:1, 0:1024], in0=c.rows[0:1, 0:1024],
        in1=c.rows[0:1, 0:1024], op=ALU.mult), inc="dv")
    t = b.op("vector", lambda e: e.tensor_tensor(
        out=c.rows[0:1, 1024:2048], in0=c.f32t[0:1, 0:1024],
        in1=psA[0:1, 0:1024], op=ALU.subtract), inc="dv")
    c.lnA_free = t
    t = b.op("scalar", lambda e: e.activation(
        out=c.f32t[0:1, 0:1024], in_=c.rows[0:1, 1024:2048], func=AF.Ln,
        bias=c.sclr[0:1, 0:1], scale=1.0), waits=[t, c.ones_tok], inc="ac")
    stats_tok = b.op("scalar", lambda e: e.activation(
        out=c.rows[0:1, 1024:2048], in_=c.f32t[0:1, 0:1024], func=AF.Exp,
        bias=0.0, scale=-0.5), waits=[t], inc="ac")
    bt = None
    for hf in range(2):
        b.op("tensor", lambda e, hf=hf: e.matmul(
            psB[:, hf * 512:(hf + 1) * 512], c.ones_rf[0:1, :],
            c.rows[0:1, hf * 512:(hf + 1) * 512], start=True, stop=True),
            waits=[stats_tok, pb_free], inc="pe")
        bt = b.op("tensor", lambda e, hf=hf: e.matmul(
            psB[:, 1024 + hf * 512:1024 + (hf + 1) * 512], c.ones_rf[0:1, :],
            c.rows[0:1, 1024 + hf * 512:1024 + (hf + 1) * 512],
            start=True, stop=True), waits=[stats_tok], inc="pe")
    last = None
    for ch in range(DC):
        b.op("vector", lambda e, ch=ch: e.tensor_tensor(
            out=c.f32t[:, :], in0=c.x_sb[:, ch * TL:(ch + 1) * TL],
            in1=psB[:, 0:1024], op=ALU.subtract), waits=[bt], inc="dv")
        t = b.op("vector", lambda e, ch=ch: e.tensor_tensor(
            out=c.a_a[:, ch * TL:(ch + 1) * TL], in0=c.f32t[:, :],
            in1=psB[:, 1024:2048], op=ALU.mult), inc="dv")
        if quant_max:
            t = b.op("vector", lambda e, ch=ch: e.tensor_reduce(
                out=c.stat[:, 0:1], in_=c.a_a[:, ch * TL:(ch + 1) * TL],
                axis=AXX, op=ALU.max, apply_absolute_value=True), inc="dv")
            if ch == 0:
                t = b.op("vector", lambda e: e.tensor_copy(
                    out=c.stat[:, 3:4], in_=c.stat[:, 0:1]), waits=[t], inc="dv")
            else:
                t = b.op("vector", lambda e: e.tensor_tensor(
                    out=c.stat[:, 2:3], in0=c.stat[:, 3:4], in1=c.stat[:, 0:1],
                    op=ALU.max), waits=[t], inc="dv")
                t = b.op("vector", lambda e: e.tensor_copy(
                    out=c.stat[:, 3:4], in_=c.stat[:, 2:3]), waits=[t], inc="dv")
        last = t
    c.xn_tok = last
    c.arena_free = [c.lnA_free, last]


def proj(c, *, wsrc, O, rhs, out_cb, ndc=DC, rhs_tok=None, extra_wait=None,
         tbw=512, ntb=2):
    b = c.b
    noc = O // 128
    slot_free = c.wring_free
    kidx = 0
    for ocp in range(0, noc, 2):
        ai = (ocp // 2) % 2
        ps = c.ps[ai]
        mm_tok = None
        for d in range(ndc):
            slot = kidx % 6
            kidx += 1
            sl = c.wring[:, slot * 512:slot * 512 + 256]
            dtok = b.op("sync", lambda e, d=d, ocp=ocp, sl=sl: wsrc(e, d, ocp, sl),
                        waits=[slot_free[slot]], inc="wl", n=16)
            for oi in range(2):
                for tb in range(ntb):
                    w = [dtok]
                    if d == 0 and oi == 0 and tb == 0:
                        w += [rhs_tok, c.arena_free[ai], extra_wait]
                    mm_tok = b.op(
                        "tensor",
                        lambda e, d=d, tb=tb, oi=oi, slot=slot, ps=ps: e.matmul(
                            ps[:, (oi * ntb + tb) * tbw:(oi * ntb + tb + 1) * tbw],
                            c.wring[:, slot * 512 + oi * 128:slot * 512 + oi * 128 + 128],
                            rhs(d, tb),
                            start=(d == 0), stop=(d == ndc - 1)),
                        waits=w, inc="pe")
            slot_free[slot] = mm_tok
        ft = None
        for oi in range(2):
            for tb in range(ntb):
                ft = out_cb(ocp + oi, tb,
                            ps[:, (oi * ntb + tb) * tbw:(oi * ntb + tb + 1) * tbw],
                            mm_tok)
        c.arena_free[ai] = ft
    return b.tok("pe")


def wsrc_scr(c, scr, ct=None):
    def f(e, d, ocp, sl):
        return e.dma_start(
            out=sl[:, 0:256],
            in_=scr[ocp * 128:ocp * 128 + 256, d * 128:(d + 1) * 128],
            transpose=True)
    return f


def wsrc_nat(c, w):
    def f(e, d, ocp, sl):
        return e.dma_start(
            out=sl[:, 0:256],
            in_=w[d * 128:(d + 1) * 128, ocp * 128:ocp * 128 + 256])
    return f


def emit_layer0(c):
    b = c.b
    cl_qkv = cast_weights(c, c.w_qkv, c.sQKV)
    cl_wo = cast_weights(c, c.w_out0, c.sWO0)
    cl_wg = cast_weights(c, c.w_g0, c.sWG0)
    cl_wu = cast_weights(c, c.w_u0, c.sWU0)
    cl_wd = cast_weights(c, c.w_d0, c.sWD0)

    ln(c)
    dbg_dump(c, "xn0", c.a_a)

    stg_free = {}

    def qkv_out(oc, tb, ps, mmtok):
        if oc < 32:
            hh = oc % 2
            sg = c.stg[:, hh * 1024 + tb * 512:hh * 1024 + (tb + 1) * 512]
            t = b.op("vector", lambda e, ps=ps, sg=sg: e.tensor_copy(
                out=sg, in_=ps), waits=[mmtok, stg_free.get(hh)], inc="dv")
            if tb == 1:
                sgf = c.stg[:, hh * 1024:hh * 1024 + 1024]
                if oc < 16:
                    st = b.op("sync", lambda e, oc=oc, sgf=sgf: e.dma_start(
                        out=c.qb[oc, :, :], in_=sgf), waits=[t], inc="st", n=16)
                else:
                    st = b.op("sync", lambda e, oc=oc, sgf=sgf: e.dma_start(
                        out=c.kl[(oc - 16) * 128:(oc - 15) * 128, :], in_=sgf),
                        waits=[t], inc="st", n=16)
                stg_free[hh] = st
            return t
        vv = oc - 32
        return b.op("vector", lambda e, ps=ps, vv=vv, tb=tb: e.tensor_copy(
            out=c.a_h[:, vv * TL + tb * 512:vv * TL + (tb + 1) * 512], in_=ps),
            waits=[mmtok, c.ln_ah_free], inc="dv")

    proj(c, wsrc=wsrc_scr(c, c.sQKV), O=6144,
         rhs=lambda d, tb: c.a_a[:, d * TL + tb * 512:d * TL + (tb + 1) * 512],
         out_cb=qkv_out, rhs_tok=c.xn_tok, extra_wait=cast_tok(cl_qkv))
    vfm_tok = b.tok("dv")

    emit_v_transpose(c, vfm_tok, v_chunks=16, vcols=None)
    kv_written = b.tok("st")

    ag = [[0, 1], [2, 3], [4, 5], [6, 7]]
    kg_tok = b.op("gpsimd", lambda e: e.collective_compute(
        "AllGather", ALU.bypass, replica_groups=ag,
        ins=[c.kl[:, :].opt()], outs=[c.kg[:, :].opt()]),
        waits=[kv_written], inc="cc")
    vg_tok = b.op("gpsimd", lambda e: e.collective_compute(
        "AllGather", ALU.bypass, replica_groups=ag,
        ins=[c.vl[:, :].opt()], outs=[c.vg[:, :].opt()]),
        waits=[kv_written], inc="cc")

    emit_ffn_swiglu(c, scr_g=c.sWG0, scr_u=c.sWU0, scr_d=c.sWD0, H=8192,
                    cast_g=cl_wg, cast_u=cl_wu, cast_d=cl_wd,
                    rhs_arena=c.a_a, rhs_tok=c.xn_tok,
                    vtr_done=getattr(c, "vtr_done", None))
    dbg_dump(c, "g0", c.a_b, waits=[c.g_tok])

    emit_attn_l0(c, kg_tok, vg_tok, cl_wo)


def emit_v_transpose(c, vfm_tok, v_chunks, vcols=None):
    b = c.b
    psB = c.psB
    slots = [1024, 1536]
    slot_free = [c.arena_free[1], c.arena_free[1]]
    f32_free = [None, None]
    for vv in range(v_chunks):
        for jt in range(NT):
            si = (vv * NT + jt) % 2
            off = slots[si]
            fsl = c.f32t[:, si * 128:(si + 1) * 128]
            tf = b.op("vector", lambda e, vv=vv, jt=jt, fsl=fsl: e.tensor_copy(
                out=fsl,
                in_=c.a_h[:, vv * TL + jt * 128:vv * TL + (jt + 1) * 128]),
                waits=[vfm_tok, f32_free[si]], inc="dv")
            tt = b.op("tensor", lambda e, off=off, fsl=fsl: e.transpose(
                psB[:, off:off + 128], fsl, c.identf[:, :]),
                waits=[tf, slot_free[si], c.const_tok], inc="pe")
            f32_free[si] = tt
            tv = b.op("vector", lambda e, jt=jt, off=off, si=si: e.tensor_copy(
                out=c.pt[:, si * 128:(si + 1) * 128], in_=psB[:, off:off + 128]),
                waits=[tt, getattr(c, "vtr_st", {}).get(si)], inc="dv")
            slot_free[si] = tv
            cb = vcols(vv) if vcols else vv * 128
            stv = b.op("sync", lambda e, jt=jt, cb=cb, si=si: e.dma_start(
                out=c.vl[jt * 128:(jt + 1) * 128, cb:cb + 128],
                in_=c.pt[:, si * 128:(si + 1) * 128]),
                waits=[tv], inc="st", n=16)
            if not hasattr(c, "vtr_st"):
                c.vtr_st = {}
            c.vtr_st[si] = stv
    c.vtr_done = b.tok("st")
    c.vtr_f32_free = b.tok("pe")
    c.arena_free[1] = b.tok("dv")
    c.pt_free = b.tok("st")
    del c.vtr_st


def emit_ffn_swiglu(c, *, scr_g, scr_u, scr_d, H, cast_g, cast_u, cast_d,
                    rhs_arena, rhs_tok, vtr_done=None):
    b = c.b
    HC = H // 128
    gcast, ucast, dcast = cast_tok(cast_g), cast_tok(cast_u), cast_tok(cast_d)
    slot_free = c.wring_free
    f32slot_free = [getattr(c, "vtr_f32_free", None)] * 2
    dn_slot_free = [None] * 6
    gated_free = None
    g_done = None
    for tb4 in range(4):
        t0 = tb4 * 256
        gated_toks = {}
        for hc in range(HC):
            ai = hc % 2
            ps = c.ps[ai]
            psg = ps[:, 0:256]
            psu = ps[:, 512:768]
            mmu = None
            for d in range(DC):
                slot = (hc * DC + d) % 6
                so = slot * 512
                dg = b.op("sync", lambda e, d=d, hc=hc, so=so: e.dma_start(
                    out=c.wring[:, so:so + 128],
                    in_=scr_g[hc * 128:(hc + 1) * 128, d * 128:(d + 1) * 128],
                    transpose=True), waits=[slot_free[slot], gcast], inc="wl", n=16)
                du = b.op("sync", lambda e, d=d, hc=hc, so=so: e.dma_start(
                    out=c.wring[:, so + 128:so + 256],
                    in_=scr_u[hc * 128:(hc + 1) * 128, d * 128:(d + 1) * 128],
                    transpose=True), waits=[ucast], inc="wl", n=16)
                w0 = [dg, rhs_tok, c.arena_free[ai]] if d == 0 else [dg]
                b.op("tensor", lambda e, d=d, so=so, psg=psg, t0=t0: e.matmul(
                    psg, c.wring[:, so:so + 128],
                    rhs_arena[:, d * TL + t0:d * TL + t0 + 256],
                    start=(d == 0), stop=(d == DC - 1)), waits=w0, inc="pe")
                mmu = b.op("tensor", lambda e, d=d, so=so, psu=psu, t0=t0: e.matmul(
                    psu, c.wring[:, so + 128:so + 256],
                    rhs_arena[:, d * TL + t0:d * TL + t0 + 256],
                    start=(d == 0), stop=(d == DC - 1)), waits=[du], inc="pe")
                slot_free[slot] = mmu
            ts = b.op("scalar", lambda e, psg=psg, hc=hc: e.activation(
                out=c.f32t[:, (hc % 2) * 256:(hc % 2) * 256 + 256], in_=psg,
                func=AF.Silu), waits=[mmu, f32slot_free[hc % 2]], inc="ac")
            tv = b.op("vector", lambda e, psu=psu, hc=hc: e.scalar_tensor_tensor(
                out=c.a_h[:, hc * 256:(hc + 1) * 256], in0=psu, scalar=1.0,
                in1=c.f32t[:, (hc % 2) * 256:(hc % 2) * 256 + 256],
                op0=ALU.mult, op1=ALU.mult),
                waits=[ts, gated_free, vtr_done if tb4 == 0 else None,
                       c.ln_ah_free if tb4 == 0 else None], inc="dv")
            f32slot_free[hc % 2] = tv
            gated_toks[hc] = tv
            c.arena_free[ai] = tv
        if tb4 == 0:
            dbg_dump(c, "gated0", c.a_h)
        for dc2 in range(DC):
            ai = dc2 % 2
            ps = c.ps[ai]
            psd = ps[:, 1024:1280]
            mmd = None
            for hc in range(HC):
                slot = (dc2 * HC + hc) % 6
                so = slot * 512 + 256
                dd = b.op("sync", lambda e, hc=hc, dc2=dc2, so=so: e.dma_start(
                    out=c.wring[:, so:so + 128],
                    in_=scr_d[dc2 * 128:(dc2 + 1) * 128, hc * 128:(hc + 1) * 128],
                    transpose=True), waits=[dn_slot_free[slot], dcast],
                    inc="wl", n=16)
                w0 = [dd, gated_toks[hc]]
                if hc == 0:
                    w0.append(c.arena_free[ai])
                mmd = b.op("tensor", lambda e, hc=hc, so=so, psd=psd: e.matmul(
                    psd, c.wring[:, so:so + 128], c.a_h[:, hc * 256:(hc + 1) * 256],
                    start=(hc == 0), stop=(hc == HC - 1)), waits=w0, inc="pe")
                dn_slot_free[slot] = mmd
            g_done = b.op("scalar", lambda e, psd=psd, dc2=dc2, t0=t0: e.activation(
                out=c.a_b[:, dc2 * TL + t0:dc2 * TL + t0 + 256], in_=psd,
                func=AF.Silu), waits=[mmd], inc="ac")
            c.arena_free[ai] = g_done
        gated_free = g_done
    c.g_tok = b.tok("ac")
    return c.g_tok


def emit_attn_l0(c, kg_tok, vg_tok, cl_wo):
    b = c.b
    psA, psB = c.psA, c.psB
    NH = 16
    SC = float(1.0 / np.sqrt(128.0))
    kh_free = [None, None]
    vh_free = [None, None]
    io = {}
    wk_ready = [b.tok("pe")]

    def load_head(h, extra=()):
        sl = h % 2
        kt = vt = None
        for g in range(16):
            rank, loc = g & 1, g >> 1
            kt = b.op("sync", lambda e, h=h, sl=sl, g=g, rank=rank, loc=loc: e.dma_start(
                out=c.kh[:, sl * 2048 + g * 128:sl * 2048 + (g + 1) * 128],
                in_=c.kg[rank * 2048 + h * 128:rank * 2048 + (h + 1) * 128,
                         loc * 128:(loc + 1) * 128]),
                waits=[kg_tok, kh_free[sl]] + list(extra) + wk_ready,
                inc="wl", n=16)
            vt = b.op("sync", lambda e, h=h, sl=sl, g=g, rank=rank, loc=loc: e.dma_start(
                out=c.vh[:, sl * 2080 + g * 128:sl * 2080 + g * 128 + 128],
                in_=c.vg[rank * 1024 + loc * 128:rank * 1024 + (loc + 1) * 128,
                         h * 128:(h + 1) * 128]),
                waits=[vg_tok, vh_free[sl]], inc="wl", n=16)
        return kt, vt

    def load_q(h, qf):
        return b.op("sync", lambda e, h=h: e.dma_start(
            out=c.qh[:, :], in_=c.qb[h, :, :]), waits=[qf], inc="wl", n=16)

    prev = None
    c.l0_trps_free = [c.arena_free[1], c.arena_free[1]]
    c.l0_avps_free = c.arena_free[1]
    psA_free = c.arena_free[0]
    io[0] = load_head(0)
    io[1] = load_head(1)
    qt = load_q(0, None)
    for h in range(NH):
        sl = h % 2
        kt, vt = io[h]
        qt_next = None
        for jt in range(NT):
            smm = None
            for kb in range(4):
                smm = b.op("tensor", lambda e, sl=sl, jt=jt, kb=kb: e.matmul(
                    psA[:, kb * 512:(kb + 1) * 512],
                    c.qh[:, jt * 128:(jt + 1) * 128],
                    c.kh[:, sl * 2048 + kb * 512:sl * 2048 + (kb + 1) * 512],
                    start=True, stop=True),
                    waits=[kt, qt, psA_free, c.g_tok] if kb == 0 else [],
                    inc="pe")
            if prev is not None:
                tail_tok = emit_l0_tail(c, *prev)
                if prev[1] == NT - 1:
                    ph = prev[0]
                    kh_free[ph % 2] = smm
                    vh_free[ph % 2] = tail_tok
                    if ph + 2 < NH:
                        io[ph + 2] = load_head(ph + 2)
            sb0 = (jt % 2) * 8
            t = b.op("vector", lambda e, sb0=sb0: e.tensor_reduce(
                out=c.stat[:, sb0:sb0 + 1], in_=psA[:, :], axis=AXX, op=ALU.max),
                waits=[smm], inc="dv")
            t = b.op("vector", lambda e, sb0=sb0: e.tensor_scalar(
                out=c.stat[:, sb0 + 1:sb0 + 2], in0=c.stat[:, sb0:sb0 + 1],
                scalar1=-SC, scalar2=None, op0=ALU.mult), waits=[t], inc="dv")
            texp = b.op("scalar", lambda e, sb0=sb0: e.activation(
                out=c.pt[:, :], in_=psA[:, :], func=AF.Exp, scale=SC,
                bias=c.stat[:, sb0 + 1:sb0 + 2],
                accum_out=c.stat[:, sb0 + 2:sb0 + 3]),
                waits=[t, getattr(c, "l0_ptr_done", None),
                       getattr(c, "pt_free", None)], inc="ac")
            c.pt_free = None
            psA_free = texp
            if "att10" in c.dbg and h == 1 and jt == 0:
                for cc in range(2):
                    tdv = b.op("vector", lambda e, cc=cc: e.tensor_copy(
                        out=c.f32t[:, :], in_=c.pt[:, cc * 1024:(cc + 1) * 1024]),
                        waits=[texp], inc="dv")
                    tst = b.op("sync", lambda e, cc=cc: e.dma_start(
                        out=c.dbg["att10"][cc, :, :], in_=c.f32t[:, :]),
                        waits=[tdv], inc="st", n=16)
                    b.wait("vector", tst)
                for cc in range(2):
                    tdv = b.op("vector", lambda e, cc=cc: e.tensor_copy(
                        out=c.f32t[:, :], in_=psA[:, cc * 1024:(cc + 1) * 1024]),
                        waits=[texp], inc="dv")
                    tst = b.op("sync", lambda e, cc=cc: e.dma_start(
                        out=c.dbg["att10"][2 + cc, :, :], in_=c.f32t[:, :]),
                        waits=[tdv], inc="st", n=16)
                    b.wait("vector", tst)
                tdv = b.op("vector", lambda e: e.tensor_copy(
                    out=c.f32t[:, :], in_=c.qh[:, :]), waits=[texp], inc="dv")
                tst = b.op("sync", lambda e: e.dma_start(
                    out=c.dbg["att10"][4, :, :], in_=c.f32t[:, :]),
                    waits=[tdv], inc="st", n=16)
                b.wait("vector", tst)
                for cc in range(2):
                    tdv = b.op("vector", lambda e, cc=cc, sl=sl: e.tensor_copy(
                        out=c.f32t[:, :],
                        in_=c.kh[:, sl * 2048 + cc * 1024:sl * 2048 + (cc + 1) * 1024]),
                        waits=[texp], inc="dv")
                    tst = b.op("sync", lambda e, cc=cc: e.dma_start(
                        out=c.dbg["att10"][5 + cc, :, :], in_=c.f32t[:, :]),
                        waits=[tdv], inc="st", n=16)
                    b.wait("vector", tst)
                tdv = b.op("vector", lambda e: e.tensor_copy(
                    out=c.f32t[:, 0:16], in_=c.stat[:, 0:16]), waits=[texp], inc="dv")
                tst = b.op("sync", lambda e: e.dma_start(
                    out=c.dbg["att10"][7, :, 0:16], in_=c.f32t[:, 0:16]),
                    waits=[tdv], inc="st", n=16)
                b.wait("vector", tst)
                psA_free = b.tok("dv")
                c.pt_free = b.tok("st")
            if jt == NT - 1:
                qt_next = load_q(h + 1, smm) if h + 1 < NH else None
            t4 = b.op("vector", lambda e, sb0=sb0: e.tensor_scalar(
                out=c.stat[:, sb0 + 4:sb0 + 5], in0=c.stat[:, sb0 + 2:sb0 + 3],
                scalar1=1e-8, scalar2=None, op0=ALU.add), waits=[texp], inc="dv")
            t5 = b.op("vector", lambda e, sb0=sb0: e.reciprocal(
                out=c.stat[:, sb0 + 5:sb0 + 6], in_=c.stat[:, sb0 + 4:sb0 + 5]),
                waits=[t4], inc="dv")
            tsc = b.op("vector", lambda e, sb0=sb0: e.tensor_scalar(
                out=c.pt[:, :], in0=c.pt[:, :],
                scalar1=c.stat[:, sb0 + 5:sb0 + 6], scalar2=None, op0=ALU.mult),
                waits=[t5], inc="dv")
            prev = (h, jt, sl, vt, tsc)
        qt = qt_next
    emit_l0_tail(c, *prev)
    c.l0_att_done = b.tok("dv")
    c.pt_free = c.l0_ptr_done
    c.arena_free = [psA_free, c.l0_att_done]
    c.wring_free = [c.l0_ptr_done] * 6
    dbg_dump(c, "av0", c.a_h, waits=[c.l0_att_done])

    def wo_out(oc, tb, ps, mmtok):
        b.op("vector", lambda e, ps=ps, oc=oc, tb=tb: e.tensor_tensor(
            out=c.f32t[:, tb * 512:(tb + 1) * 512], in0=ps,
            in1=c.a_b[:, oc * TL + tb * 512:oc * TL + (tb + 1) * 512],
            op=ALU.mult), waits=[mmtok, c.g_tok], inc="dv")
        t = b.op("vector", lambda e, oc=oc, tb=tb: e.tensor_tensor(
            out=c.x_sb[:, oc * TL + tb * 512:oc * TL + (tb + 1) * 512],
            in0=c.f32t[:, tb * 512:(tb + 1) * 512],
            in1=c.x_sb[:, oc * TL + tb * 512:oc * TL + (tb + 1) * 512],
            op=ALU.add), inc="dv")
        if tb == 1:
            c.combine_toks[oc] = t
            c.combine_last = t
        return t

    proj(c, wsrc=wsrc_scr(c, c.sWO0), O=2048,
         rhs=lambda d, tb: c.a_h[:, d * TL + tb * 512:d * TL + (tb + 1) * 512],
         out_cb=wo_out, rhs_tok=c.l0_att_done, extra_wait=cast_tok(cl_wo))


def emit_l0_tail(c, h, jt, sl, vt, tsc):
    b = c.b
    psB = c.psB
    boffs = [1024, 1536]
    ma = None
    for tc in range(16):
        si = tc % 2
        off = boffs[si]
        pslot = (tc % 4) * 128
        tt = b.op("tensor", lambda e, tc=tc, off=off: e.transpose(
            psB[:, off:off + 64].bitcast(BF16),
            c.pt[:, tc * 128:(tc + 1) * 128], c.identb[:, :]),
            waits=[tsc, c.l0_trps_free[si], c.const_tok], inc="pe")
        tv = b.op("vector", lambda e, off=off, pslot=pslot: e.tensor_copy(
            out=c.ptT[:, pslot:pslot + 128],
            in_=psB[:, off:off + 64].bitcast(BF16)),
            waits=[tt, getattr(c, "l0_ptT_free_%d" % (tc % 4), None)], inc="dv")
        c.l0_trps_free[si] = tv
        ma = b.op("tensor", lambda e, tc=tc, sl=sl, pslot=pslot: e.matmul(
            psB[:, 0:128],
            c.vh[:, sl * 2080 + tc * 128:sl * 2080 + tc * 128 + 128],
            c.ptT[:, pslot:pslot + 128],
            start=(tc == 0), stop=(tc == 15)),
            waits=[tv, vt] + ([getattr(c, "l0_avps_free", None)] if tc == 0 else []),
            inc="pe")
        setattr(c, "l0_ptT_free_%d" % (tc % 4), ma)
    c.l0_ptr_done = ma
    t = b.op("vector", lambda e, h=h, jt=jt: e.tensor_copy(
        out=c.a_h[:, h * TL + jt * 128:h * TL + (jt + 1) * 128],
        in_=psB[:, 0:128]), waits=[ma], inc="dv")
    c.l0_avps_free = t
    return t


def rowmax_to_scalar(c, stat_col, dst_off, waits):
    b = c.b
    t = b.op("tensor", lambda e, stat_col=stat_col: e.matmul(
        c.psA[0:1, 1536:1664], c.stat[:, stat_col:stat_col + 1], c.identf[:, :],
        start=True, stop=True),
        waits=list(waits) + [getattr(c, "mb_free", None), c.const_tok], inc="pe")
    t = b.op("vector", lambda e, dst_off=dst_off: e.tensor_reduce(
        out=c.rows[0:1, dst_off:dst_off + 1], in_=c.psA[0:1, 1536:1664],
        axis=AXX, op=ALU.max), waits=[t], inc="dv")
    c.mb_free = t
    return t


def bcast_scalar(c, src_off, dst_col, waits, scale=1.0, extra=()):
    b = c.b
    t = b.op("tensor", lambda e, src_off=src_off: e.matmul(
        c.psA[:, 1664:1665], c.ones_rf[0:1, :],
        c.rows[0:1, src_off:src_off + 1], start=True, stop=True),
        waits=list(waits) + [getattr(c, "mb_free", None)], inc="pe")
    t = b.op("vector", lambda e, dst_col=dst_col, scale=scale: e.tensor_scalar(
        out=c.sclr[:, dst_col:dst_col + 1], in0=c.psA[:, 1664:1665],
        scalar1=scale, scalar2=None, op0=ALU.mult),
        waits=[t] + list(extra), inc="dv")
    c.mb_free = t
    return t


def scalar_allreduce_max(c, src_off, dst_off, ar_in, ar_out, waits):
    b = c.b
    t = b.op("gpsimd", lambda e, src_off=src_off: e.dma_start(
        out=ar_in[:, :], in_=c.rows[0:1, src_off:src_off + 1]),
        waits=list(waits), inc="xg", n=16)
    t = b.op("gpsimd", lambda e: e.collective_compute(
        "AllReduce", ALU.max, replica_groups=[[0, 1, 2, 3, 4, 5, 6, 7]],
        ins=[ar_in[:, :].opt()], outs=[ar_out[:, :].opt()]),
        waits=[t], inc="cc")
    t = b.op("gpsimd", lambda e, dst_off=dst_off: e.dma_start(
        out=c.rows[0:1, dst_off:dst_off + 1], in_=ar_out[:, :]),
        waits=[t], inc="xg", n=16)
    return t


def quant_rows(c, wsrc_dram, rows_n, cols_n, qdst, scl_dst, scl_row0, waits=()):
    b = c.b
    last = None
    for r0 in range(0, rows_n, 128):
        nh = cols_n // 512
        for qf in range(nh):
            t = b.op("sync", lambda e, r0=r0, qf=qf: e.dma_start(
                out=c.f32t[:, 0:512],
                in_=wsrc_dram[r0:r0 + 128, qf * 512:(qf + 1) * 512]),
                waits=list(waits) + [getattr(c, "q_f32t_free", None)],
                inc="wl", n=16)
            t = b.op("vector", lambda e: e.tensor_reduce(
                out=c.stat[:, 0:1], in_=c.f32t[:, 0:512], axis=AXX, op=ALU.max,
                apply_absolute_value=True), waits=[t], inc="dv")
            if qf == 0:
                t = b.op("vector", lambda e: e.tensor_copy(
                    out=c.stat[:, 1:2], in_=c.stat[:, 0:1]), waits=[t], inc="dv")
            else:
                t = b.op("vector", lambda e: e.tensor_tensor(
                    out=c.stat[:, 1:2], in0=c.stat[:, 1:2], in1=c.stat[:, 0:1],
                    op=ALU.max), waits=[t], inc="dv")
            c.q_f32t_free = t
        t = b.op("vector", lambda e: e.tensor_scalar(
            out=c.stat[:, 2:3], in0=c.stat[:, 1:2], scalar1=1.0 / 127.0,
            scalar2=1e-8, op0=ALU.mult, op1=ALU.max), waits=[t], inc="dv")
        t = b.op("vector", lambda e: e.reciprocal(
            out=c.stat[:, 3:4], in_=c.stat[:, 2:3]), waits=[t], inc="dv")
        rsc_tok = t
        t = b.op("sync", lambda e, r0=r0: e.dma_start(
            out=scl_dst[scl_row0 + r0:scl_row0 + r0 + 128, :],
            in_=c.stat[:, 2:3]), waits=[t], inc="st", n=16)
        scl_st = t
        for qf in range(nh):
            t = b.op("sync", lambda e, r0=r0, qf=qf: e.dma_start(
                out=c.f32t[:, 0:512],
                in_=wsrc_dram[r0:r0 + 128, qf * 512:(qf + 1) * 512]),
                waits=[getattr(c, "q_f32t_free", None), scl_st], inc="wl", n=16)
            t = b.op("vector", lambda e: e.tensor_scalar(
                out=c.f32t[:, 512:1024], in0=c.f32t[:, 0:512],
                scalar1=c.stat[:, 3:4], scalar2=None, op0=ALU.mult),
                waits=[t, rsc_tok], inc="dv")
            t = b.op("vector", lambda e: e.tensor_copy(
                out=c.f32t[:, 0:512].bitcast(I32), in_=c.f32t[:, 512:1024]),
                waits=[t], inc="dv")
            t = b.op("vector", lambda e: e.tensor_copy(
                out=c.pp[:, 0:512], in_=c.f32t[:, 0:512].bitcast(I32)),
                waits=[t, getattr(c, "q_pp_free", None)], inc="dv")
            c.q_f32t_free = t
            t = b.op("vector", lambda e: e.tensor_scalar(
                out=c.pp[:, 0:512], in0=c.pp[:, 0:512], scalar1=127.0,
                scalar2=-127.0, op0=ALU.min, op1=ALU.max), waits=[t], inc="dv")
            t = b.op("sync", lambda e, r0=r0, qf=qf: e.dma_start(
                out=qdst[r0:r0 + 128, qf * 512:(qf + 1) * 512],
                in_=c.pp[:, 0:512]), waits=[t], inc="st", n=16)
            c.q_pp_free = t
            last = t
    return last


def ln2(c):
    b = c.b
    psA, psB = c.psA, c.psB
    sq_mm = [None, None]
    pa_free = c.arena_free[0]
    pb_free = c.arena_free[1]
    for ch in range(DC):
        half0 = (ch % 2) * 1024
        t = b.op("vector", lambda e, ch=ch, half0=half0: e.tensor_tensor(
            out=c.pt[:, half0:half0 + TL],
            in0=c.a_a[:, ch * TL:(ch + 1) * TL],
            in1=c.a_a[:, ch * TL:(ch + 1) * TL], op=ALU.mult),
            waits=[c.xn_tok, sq_mm[ch % 2], getattr(c, "pt_free", None)], inc="dv")
        for hf in range(2):
            b.op("tensor", lambda e, ch=ch, hf=hf: e.matmul(
                psA[0:1, hf * 512:(hf + 1) * 512], c.ones_c[:, 0:1],
                c.a_a[:, ch * TL + hf * 512:ch * TL + (hf + 1) * 512],
                start=(ch == 0), stop=(ch == DC - 1)),
                waits=[c.xn_tok, c.ones_tok, pa_free], inc="pe")
            t2 = b.op("tensor", lambda e, ch=ch, hf=hf, half0=half0: e.matmul(
                psA[0:1, 1024 + hf * 512:1024 + (hf + 1) * 512], c.ones_c[:, 0:1],
                c.pt[:, half0 + hf * 512:half0 + (hf + 1) * 512],
                start=(ch == 0), stop=(ch == DC - 1)),
                waits=[t], inc="pe")
        sq_mm[ch % 2] = t2
    mm_done = b.tok("pe")
    c.pt_free = None
    b.op("vector", lambda e: e.tensor_scalar(
        out=c.rows[0:1, 0:1024], in0=psA[0:1, 0:1024], scalar1=1.0 / 2048.0,
        scalar2=None, op0=ALU.mult), waits=[mm_done], inc="dv")
    b.op("vector", lambda e: e.tensor_scalar(
        out=c.f32t[0:1, 0:1024], in0=psA[0:1, 1024:2048], scalar1=1.0 / 2048.0,
        scalar2=None, op0=ALU.mult), inc="dv")
    b.op("vector", lambda e: e.tensor_tensor(
        out=psA[0:1, 0:1024], in0=c.rows[0:1, 0:1024],
        in1=c.rows[0:1, 0:1024], op=ALU.mult), inc="dv")
    t = b.op("vector", lambda e: e.tensor_tensor(
        out=c.rows[0:1, 1024:2048], in0=c.f32t[0:1, 0:1024],
        in1=psA[0:1, 0:1024], op=ALU.subtract), inc="dv")
    c.lnA_free = t
    t = b.op("scalar", lambda e: e.activation(
        out=c.f32t[0:1, 0:1024], in_=c.rows[0:1, 1024:2048], func=AF.Ln,
        bias=c.sclr[0:1, 0:1], scale=1.0), waits=[t, c.ones_tok], inc="ac")
    stats_tok = b.op("scalar", lambda e: e.activation(
        out=c.rows[0:1, 1024:2048], in_=c.f32t[0:1, 0:1024], func=AF.Exp,
        bias=0.0, scale=-0.5), waits=[t], inc="ac")
    bt = None
    for hf in range(2):
        b.op("tensor", lambda e, hf=hf: e.matmul(
            psB[:, hf * 512:(hf + 1) * 512], c.ones_rf[0:1, :],
            c.rows[0:1, hf * 512:(hf + 1) * 512], start=True, stop=True),
            waits=[stats_tok, pb_free], inc="pe")
        bt = b.op("tensor", lambda e, hf=hf: e.matmul(
            psB[:, 1024 + hf * 512:1024 + (hf + 1) * 512], c.ones_rf[0:1, :],
            c.rows[0:1, 1024 + hf * 512:1024 + (hf + 1) * 512],
            start=True, stop=True), waits=[stats_tok], inc="pe")
    gcol = BPC["l1_ln_g"]
    bcol = BPC["l1_ln_b"]
    last = None
    for ch in range(DC):
        b.op("vector", lambda e, ch=ch: e.tensor_tensor(
            out=c.f32t[:, :], in0=c.a_a[:, ch * TL:(ch + 1) * TL],
            in1=psB[:, 0:1024], op=ALU.subtract), waits=[bt], inc="dv")
        b.op("vector", lambda e: e.tensor_tensor(
            out=c.f32t[:, :], in0=c.f32t[:, :], in1=psB[:, 1024:2048],
            op=ALU.mult), inc="dv")
        last = b.op("vector", lambda e, ch=ch: e.tensor_scalar(
            out=c.a_b[:, ch * TL:(ch + 1) * TL], in0=c.f32t[:, :],
            scalar1=c.bp[:, gcol + ch:gcol + ch + 1],
            scalar2=c.bp[:, bcol + ch:bcol + ch + 1],
            op0=ALU.mult, op1=ALU.add), inc="dv")
    c.Xn_tok = last
    c.arena_free = [c.lnA_free, last]


def quantize_arena(c, rscale_col, n_chunks=DC):
    b = c.b
    last = None
    for ch in range(n_chunks):
        for hf in range(2):
            sl = slice(ch * TL + hf * 512, ch * TL + (hf + 1) * 512)
            t = b.op("vector", lambda e, sl=sl: e.tensor_scalar(
                out=c.f32t[:, 512:1024], in0=c.a_a[:, sl],
                scalar1=c.sclr[:, rscale_col:rscale_col + 1], scalar2=None,
                op0=ALU.mult),
                waits=[c.xn_tok, getattr(c, "qz_free", None)], inc="dv")
            t = b.op("vector", lambda e: e.tensor_copy(
                out=c.f32t[:, 0:512].bitcast(I32), in_=c.f32t[:, 512:1024]),
                waits=[t], inc="dv")
            t = b.op("vector", lambda e, sl=sl: e.tensor_copy(
                out=c.a_a[:, sl], in_=c.f32t[:, 0:512].bitcast(I32)),
                waits=[t], inc="dv")
            c.qz_free = t
            last = b.op("vector", lambda e, sl=sl: e.tensor_scalar(
                out=c.a_a[:, sl], in0=c.a_a[:, sl], scalar1=127.0,
                scalar2=-127.0, op0=ALU.min, op1=ALU.max), waits=[t], inc="dv")
    c.qx_tok = last
    return last


def emit_layer1(c):
    b = c.b
    psA, psB = c.psA, c.psB
    cl_q = cast_weights(c, c.w1q, c.s1q)
    cl_k = cast_weights(c, c.w1k, c.s1k)
    cl_v = cast_weights(c, c.w1v, c.s1v)
    cl_o = cast_weights(c, c.w1o, c.s1o)
    qb1 = quant_rows(c, c.b_w1s, 1024, 2048, c.qw1l, c.scl1l, 0)
    qb2 = quant_rows(c, c.b_w2s, 256, 8192, c.qw2l, c.scl1l, 1024,
                     waits=[qb1])
    ag = [[0, 1], [2, 3], [4, 5], [6, 7]]
    all8 = [[0, 1, 2, 3, 4, 5, 6, 7]]
    g1 = b.op("gpsimd", lambda e: e.collective_compute(
        "AllGather", ALU.bypass, replica_groups=all8,
        ins=[c.qw1l[:, :].opt()], outs=[c.qw1g[:, :].opt()]),
        waits=[qb2], inc="cc")
    g2 = b.op("gpsimd", lambda e: e.collective_compute(
        "AllGather", ALU.bypass, replica_groups=all8,
        ins=[c.qw2l[:, :].opt()], outs=[c.qw2g[:, :].opt()]),
        waits=[qb2], inc="cc")
    g3 = b.op("gpsimd", lambda e: e.collective_compute(
        "AllGather", ALU.bypass, replica_groups=all8,
        ins=[c.scl1l[:, :].opt()], outs=[c.scl1g[:, :].opt()]),
        waits=[qb2], inc="cc")

    ln(c, quant_max=True)
    dbg_dump(c, "xn1", c.a_a)
    t = rowmax_to_scalar(c, 3, 2052, [c.xn_tok, c.arena_free[0]])
    ar1 = scalar_allreduce_max(c, 2052, 2053, c.ar1i, c.ar1o, [t])
    t = b.op("vector", lambda e: e.tensor_scalar(
        out=c.rows[0:1, 2054:2055], in0=c.rows[0:1, 2053:2054],
        scalar1=1.0 / 127.0, scalar2=1e-8, op0=ALU.mult, op1=ALU.max),
        waits=[ar1], inc="dv")
    t = bcast_scalar(c, 2054, 1, [t])
    t5 = b.op("vector", lambda e: e.reciprocal(
        out=c.sclr[:, 2:3], in_=c.sclr[:, 1:2]), waits=[t], inc="dv")

    ln2(c)

    def scl_rows_w1(hc):
        grow = hc * 128
        rank, idx = grow // 1024, grow % 1024
        return rank * 1280 + idx
    def scl_rows_w2(dc2):
        grow = dc2 * 128
        rank, idx = grow // 256, grow % 256
        return rank * 1280 + 1024 + idx
    sc_toks = {}
    for hc in range(64):
        r = scl_rows_w1(hc)
        sc_toks[hc] = b.op("sync", lambda e, r=r, hc=hc: e.dma_start(
            out=c.sclr[:, 8 + hc:9 + hc], in_=c.scl1g[r:r + 128, :]),
            waits=[g1, g3], inc="wl", n=16)
    for dc2 in range(DC):
        r = scl_rows_w2(dc2)
        sc_toks[64 + dc2] = b.op("sync", lambda e, r=r, dc2=dc2: e.dma_start(
            out=c.sclr[:, 72 + dc2:73 + dc2], in_=c.scl1g[r:r + 128, :]),
            waits=[g2, g3], inc="wl", n=16)
    sxs_tok = {}
    for hc in range(64):
        sxs_tok[hc] = b.op("vector", lambda e, hc=hc: e.tensor_scalar(
            out=c.sclr[:, 8 + hc:9 + hc], in0=c.sclr[:, 8 + hc:9 + hc],
            scalar1=c.sclr[:, 1:2], scalar2=None, op0=ALU.mult),
            waits=[sc_toks[hc], t5], inc="dv")
    for dc2 in range(DC):
        sxs_tok[64 + dc2] = sc_toks[64 + dc2]

    def wsrc_nat_scr(scr):
        def f(e, d, ocp, sl):
            return e.dma_start(out=sl[:, 0:256],
                               in_=scr[d * 128:(d + 1) * 128,
                                       ocp * 128:ocp * 128 + 256])
        return f

    stg_free = {0: qb2, 1: qb2}

    def mk_qkv_out(dst, bias_key):
        bcol = BPC[bias_key]

        def qkv_out(oc, tb, ps, mmtok):
            hh = oc % 2
            sg = c.pp[:, hh * 1024 + tb * 512:hh * 1024 + (tb + 1) * 512]
            t = b.op("vector", lambda e, ps=ps, sg=sg, oc=oc: e.tensor_scalar(
                out=sg, in0=ps, scalar1=c.bp[:, bcol + oc:bcol + oc + 1],
                scalar2=None, op0=ALU.add),
                waits=[mmtok, stg_free.get(hh)], inc="dv")
            if tb == 1:
                sgf = c.pp[:, hh * 1024:hh * 1024 + 1024]
                if dst == "q":
                    st = b.op("sync", lambda e, oc=oc, sgf=sgf: e.dma_start(
                        out=c.qb[oc, :, :], in_=sgf), waits=[t], inc="st", n=16)
                elif dst == "k":
                    st = b.op("sync", lambda e, oc=oc, sgf=sgf: e.dma_start(
                        out=c.kl[oc * 128:(oc + 1) * 128, :], in_=sgf),
                        waits=[t], inc="st", n=16)
                stg_free[hh] = st
            if dst == "v" and tb == 1:
                pass
            return t

        def qkv_out_v(oc, tb, ps, mmtok):
            return b.op("vector", lambda e, ps=ps, oc=oc, tb=tb: e.tensor_scalar(
                out=c.a_h[:, oc * TL + tb * 512:oc * TL + (tb + 1) * 512],
                in0=ps, scalar1=c.bp[:, bcol + oc:bcol + oc + 1],
                scalar2=None, op0=ALU.add),
                waits=[mmtok, c.ln_ah_free], inc="dv")
        return qkv_out_v if dst == "v" else qkv_out

    proj(c, wsrc=wsrc_nat_scr(c.s1q), O=2048,
         rhs=lambda d, tb: c.a_b[:, d * TL + tb * 512:d * TL + (tb + 1) * 512],
         out_cb=mk_qkv_out("q", "l1_bq"), rhs_tok=c.Xn_tok, extra_wait=cast_tok(cl_q))
    proj(c, wsrc=wsrc_nat_scr(c.s1k), O=2048,
         rhs=lambda d, tb: c.a_b[:, d * TL + tb * 512:d * TL + (tb + 1) * 512],
         out_cb=mk_qkv_out("k", "l1_bk"), rhs_tok=c.Xn_tok, extra_wait=cast_tok(cl_k))
    proj(c, wsrc=wsrc_nat_scr(c.s1v), O=2048,
         rhs=lambda d, tb: c.a_b[:, d * TL + tb * 512:d * TL + (tb + 1) * 512],
         out_cb=mk_qkv_out("v", "l1_bv"), rhs_tok=c.Xn_tok, extra_wait=cast_tok(cl_v))
    vfm_tok = b.tok("dv")
    emit_v_transpose(c, vfm_tok, v_chunks=16)
    kv_written = b.tok("st")
    kg_tok = b.op("gpsimd", lambda e: e.collective_compute(
        "AllGather", ALU.bypass, replica_groups=ag,
        ins=[c.kl[:, :].opt()], outs=[c.kg[:, :].opt()]),
        waits=[kv_written], inc="cc")
    vg_tok = b.op("gpsimd", lambda e: e.collective_compute(
        "AllGather", ALU.bypass, replica_groups=ag,
        ins=[c.vl[:, :].opt()], outs=[c.vg[:, :].opt()]),
        waits=[kv_written], inc="cc")

    quantize_arena(c, 2)
    dbg_dump(c, "qx1", c.a_a, waits=[c.qx_tok])

    b1col = BPC["l1_b1"]
    hstat = {"tok": None, "init": False}
    f32_free = {}

    def mm1_out(oc, tb, ps, mmtok):
        t = b.op("vector", lambda e, ps=ps, oc=oc: e.tensor_scalar(
            out=c.f32t[:, 0:512], in0=ps,
            scalar1=c.sclr[:, 8 + oc:9 + oc],
            scalar2=c.bp[:, b1col + oc:b1col + oc + 1],
            op0=ALU.mult, op1=ALU.add),
            waits=[mmtok, sxs_tok[oc], f32_free.get(0)], inc="dv")
        t = b.op("scalar", lambda e: e.activation(
            out=c.f32t[:, 512:1024], in_=c.f32t[:, 0:512], func=AF.Gelu),
            waits=[t, f32_free.get(1)], inc="ac")
        f32_free[0] = t
        st = b.op("sync", lambda e, oc=oc, tb=tb: e.dma_start(
            out=c.h1[oc, :, tb * 512:(tb + 1) * 512], in_=c.f32t[:, 512:1024]),
            waits=[t], inc="st", n=16)
        t = b.op("vector", lambda e: e.tensor_reduce(
            out=c.stat[:, 4:5], in_=c.f32t[:, 512:1024], axis=AXX, op=ALU.max,
            apply_absolute_value=True), waits=[t], inc="dv")
        if not hstat["init"]:
            t = b.op("vector", lambda e: e.tensor_copy(
                out=c.stat[:, 6:7], in_=c.stat[:, 4:5]), waits=[t], inc="dv")
            hstat["init"] = True
        else:
            t = b.op("vector", lambda e: e.tensor_tensor(
                out=c.stat[:, 6:7], in0=c.stat[:, 6:7], in1=c.stat[:, 4:5],
                op=ALU.max), waits=[t], inc="dv")
        f32_free[1] = b.op("vector", lambda e: e.tensor_copy(
            out=c.stat[:, 5:6], in_=c.stat[:, 4:5]), waits=[st, t], inc="dv")
        hstat["tok"] = t
        return f32_free[1]

    def wsrc_tr(scr):
        def f(e, d, ocp, sl):
            return e.dma_start(out=sl[:, 0:256],
                               in_=scr[ocp * 128:ocp * 128 + 256,
                                       d * 128:(d + 1) * 128], transpose=True)
        return f

    proj(c, wsrc=wsrc_tr(c.qw1g), O=8192,
         rhs=lambda d, tb: c.a_a[:, d * TL + tb * 512:d * TL + (tb + 1) * 512],
         out_cb=mm1_out, rhs_tok=c.qx_tok, extra_wait=g1)
    if "h1d32" in c.dbg:
        for ch in range(DC):
            b.op("sync", lambda e, ch=ch: e.dma_start(
                out=c.dbg["h1d32"][ch, :, :], in_=c.h1[ch, :, :]),
                inc="st", n=16)
    t = rowmax_to_scalar(c, 6, 2056, [hstat["tok"], c.arena_free[0]])
    ar2 = scalar_allreduce_max(c, 2056, 2057, c.ar2i, c.ar2o, [t])
    t = b.op("vector", lambda e: e.tensor_scalar(
        out=c.rows[0:1, 2058:2059], in0=c.rows[0:1, 2057:2058],
        scalar1=1.0 / 127.0, scalar2=1e-8, op0=ALU.mult, op1=ALU.max),
        waits=[ar2], inc="dv")
    t = bcast_scalar(c, 2058, 3, [t])
    rs2_tok = b.op("vector", lambda e: e.reciprocal(
        out=c.sclr[:, 4:5], in_=c.sclr[:, 3:4]), waits=[t], inc="dv")
    sxs2_tok = {}
    for dc2 in range(DC):
        sxs2_tok[dc2] = b.op("vector", lambda e, dc2=dc2: e.tensor_scalar(
            out=c.sclr[:, 72 + dc2:73 + dc2], in0=c.sclr[:, 72 + dc2:73 + dc2],
            scalar1=c.sclr[:, 3:4], scalar2=None, op0=ALU.mult),
            waits=[sxs_tok[64 + dc2], t], inc="dv")

    emit_attn_l1(c, kg_tok, vg_tok)

    emit_l1_mm2(c, rs2_tok, sxs2_tok, g2)


def emit_attn_l1(c, kg_tok, vg_tok):
    b = c.b
    psA, psB = c.psA, c.psB
    NH = 16
    SC = float(1.0 / np.sqrt(128.0))
    kh_free = [None, None]
    vh_free = [None, None]
    wk_ready = [b.tok("pe")]
    io = {}

    def load_head(h):
        sl = h % 2
        kt = vt = None
        for g in range(16):
            rank, loc = g & 1, g >> 1
            kt = b.op("sync", lambda e, h=h, sl=sl, g=g, rank=rank, loc=loc: e.dma_start(
                out=c.kh[:, sl * 2048 + g * 128:sl * 2048 + (g + 1) * 128],
                in_=c.kg[rank * 2048 + h * 128:rank * 2048 + (h + 1) * 128,
                         loc * 128:(loc + 1) * 128]),
                waits=[kg_tok, kh_free[sl]] + wk_ready, inc="wl", n=16)
            vt = b.op("sync", lambda e, h=h, sl=sl, g=g, rank=rank, loc=loc: e.dma_start(
                out=c.vh[:, sl * 2080 + g * 128:sl * 2080 + g * 128 + 128],
                in_=c.vg[rank * 1024 + loc * 128:rank * 1024 + (loc + 1) * 128,
                         h * 128:(h + 1) * 128]),
                waits=[vg_tok, vh_free[sl]], inc="wl", n=16)
        return kt, vt

    def load_q(h, qf):
        return b.op("sync", lambda e, h=h: e.dma_start(
            out=c.qh[:, :], in_=c.qb[h, :, :]), waits=[qf], inc="wl", n=16)

    io[0] = load_head(0)
    io[1] = load_head(1)
    qt = load_q(0, None)
    sslot_free = [c.arena_free[0]] * 3
    cexp_prev = [None, None]
    pp_free = [None] * 4
    mk_free = [None, None]
    av_free = c.arena_free[1]
    den_free = c.arena_free[1]
    rb_free = c.arena_free[1]
    ah_w = getattr(c, "vtr_done", None)
    last_smm = None
    for h in range(NH):
        sl = h % 2
        kt, vt = io[h]
        qt_next = None
        for tb in range(2):
            ks = list(range(0, 8 * tb + 8))
            order = [ks[-1]] + ks[:-1]
            nk = len(order)
            first_mm = {}
            exp_toks = []
            av_mm = None
            den_mm = None
            for idx, kk in enumerate(order):
                ss = (h * 2 + tb + idx) % 3
                smm = b.op("tensor", lambda e, sl=sl, tb=tb, kk=kk, ss=ss: e.matmul(
                    psA[:, ss * 512:(ss + 1) * 512],
                    c.kh[:, sl * 2048 + kk * 128:sl * 2048 + (kk + 1) * 128],
                    c.qh[:, tb * 512:(tb + 1) * 512], start=True, stop=True),
                    waits=[kt, qt, sslot_free[ss]], inc="pe")
                last_smm = smm
                if idx == 0:
                    pc = (h * 2 + tb) % 2
                    t = b.op("vector", lambda e, ss=ss, pc=pc: e.tensor_reduce(
                        out=c.stat[:, 8 + pc:9 + pc],
                        in_=psA[:, ss * 512:(ss + 1) * 512],
                        axis=AXX, op=ALU.max), waits=[smm], inc="dv")
                    t = rowmax_to_scalar(c, 8 + pc, 2060 + pc, [t])
                    cb = bcast_scalar(c, 2060 + pc, 5 + pc, [t], scale=-SC,
                                      extra=[cexp_prev[pc]])
                pslot = idx % 4
                texp = b.op("scalar", lambda e, ss=ss, pslot=pslot, pc=pc: e.activation(
                    out=c.pp[:, pslot * 512:(pslot + 1) * 512],
                    in_=psA[:, ss * 512:(ss + 1) * 512], func=AF.Exp, scale=SC,
                    bias=c.sclr[:, 5 + pc:6 + pc]),
                    waits=[smm, cb, pp_free[pslot]], inc="ac")
                cexp_prev[pc] = texp
                sslot_free[ss] = texp
                ptok = texp
                if kk >= 8 * tb:
                    midx = tb * 8 + (kk - 8 * tb)
                    ms = idx % 2
                    mt = b.op("sync", lambda e, midx=midx, ms=ms: e.dma_start(
                        out=c.mk[:, ms * 512:(ms + 1) * 512],
                        in_=c.m1_d[midx, :, :]),
                        waits=[mk_free[ms]], inc="wl", n=16)
                    ptok = b.op("vector", lambda e, pslot=pslot, ms=ms: e.tensor_tensor(
                        out=c.pp[:, pslot * 512:(pslot + 1) * 512],
                        in0=c.pp[:, pslot * 512:(pslot + 1) * 512],
                        in1=c.mk[:, ms * 512:(ms + 1) * 512], op=ALU.mult),
                        waits=[texp, mt], inc="dv")
                    mk_free[ms] = ptok
                den_mm = b.op("tensor", lambda e, pslot=pslot, idx=idx, nk=nk: e.matmul(
                    psB[0:1, 512:1024], c.ones_c[:, 0:1],
                    c.pp[:, pslot * 512:(pslot + 1) * 512],
                    start=(idx == 0), stop=(idx == nk - 1)),
                    waits=[ptok, c.ones_tok] + ([den_free] if idx == 0 else []),
                    inc="pe")
                av_mm = b.op("tensor", lambda e, pslot=pslot, sl=sl, kk=kk, idx=idx, nk=nk: e.matmul(
                    psB[:, 0:512],
                    c.vh[:, sl * 2080 + kk * 128:sl * 2080 + kk * 128 + 128],
                    c.pp[:, pslot * 512:(pslot + 1) * 512],
                    start=(idx == 0), stop=(idx == nk - 1)),
                    waits=[ptok, vt] + ([av_free] if idx == 0 else []), inc="pe")
                pp_free[pslot] = av_mm
            t = b.op("vector", lambda e: e.reciprocal(
                out=c.rows[0:1, 1024:1536], in_=psB[0:1, 512:1024]),
                waits=[den_mm, rb_free], inc="dv")
            den_free = t
            t = b.op("tensor", lambda e: e.matmul(
                psB[:, 1536:2048], c.ones_rf[0:1, :], c.rows[0:1, 1024:1536],
                start=True, stop=True), waits=[t, rb_free], inc="pe")
            t = b.op("vector", lambda e: e.tensor_copy(
                out=c.mk[:, 0:512], in_=psB[:, 1536:2048]),
                waits=[t, mk_free[0]], inc="dv")
            rb_free = t
            mk_free[0] = t
            t = b.op("vector", lambda e, h=h, tb=tb: e.tensor_tensor(
                out=c.a_h[:, h * TL + tb * 512:h * TL + (tb + 1) * 512],
                in0=psB[:, 0:512], in1=c.mk[:, 0:512], op=ALU.mult),
                waits=[av_mm, t, ah_w], inc="dv")
            av_free = t
            mk_free[0] = t
            if tb == 1:
                qt_next = load_q(h + 1, last_smm) if h + 1 < NH else None
                kh_free[sl] = last_smm
                vh_free[sl] = av_mm
                if h + 2 < NH:
                    io[h + 2] = load_head(h + 2)
        qt = qt_next
    c.l1_att_done = b.tok("dv")
    c.arena_free = [b.tok("ac"), c.l1_att_done]
    c.wring_free = [b.tok("pe")] * 6
    c.pt_free = None

    bocol = BPC["l1_bo"]

    def wo_out(oc, tb, ps, mmtok):
        t = b.op("vector", lambda e, ps=ps, oc=oc, tb=tb: e.tensor_scalar(
            out=c.a_b[:, oc * TL + tb * 512:oc * TL + (tb + 1) * 512],
            in0=ps, scalar1=c.bp[:, bocol + oc:bocol + oc + 1], scalar2=None,
            op0=ALU.add), waits=[mmtok, c.Xn_tok], inc="dv")
        return t

    def wsrc_nat_scr2(e, d, ocp, sl):
        return e.dma_start(out=sl[:, 0:256],
                           in_=c.s1o[d * 128:(d + 1) * 128,
                                     ocp * 128:ocp * 128 + 256])

    proj(c, wsrc=wsrc_nat_scr2, O=2048,
         rhs=lambda d, tb: c.a_h[:, d * TL + tb * 512:d * TL + (tb + 1) * 512],
         out_cb=wo_out, rhs_tok=c.l1_att_done)
    c.s_tok = b.tok("dv")
    dbg_dump(c, "s1", c.a_b, waits=[c.s_tok])


def emit_l1_mm2(c, rs2_tok, sxs2_tok, g2):
    b = c.b
    psA = c.psA
    b2col = BPC["l1_b2"]
    HC = 64
    slot_free = c.wring_free
    qh_toks = {}
    last = None
    for tb4 in range(4):
        t0 = tb4 * 256
        for hc in range(HC):
            t = b.op("sync", lambda e, hc=hc, t0=t0: e.dma_start(
                out=c.f32t[:, 0:256], in_=c.h1[hc, :, t0:t0 + 256]),
                waits=[getattr(c, "l1q_f32_free", None)], inc="wl", n=16)
            t = b.op("vector", lambda e: e.tensor_scalar(
                out=c.f32t[:, 256:512], in0=c.f32t[:, 0:256],
                scalar1=c.sclr[:, 4:5], scalar2=None, op0=ALU.mult),
                waits=[t, rs2_tok], inc="dv")
            t = b.op("vector", lambda e: e.tensor_copy(
                out=c.f32t[:, 0:256].bitcast(I32), in_=c.f32t[:, 256:512]),
                waits=[t], inc="dv")
            t = b.op("vector", lambda e, hc=hc: e.tensor_copy(
                out=c.a_h[:, hc * 256:(hc + 1) * 256],
                in_=c.f32t[:, 0:256].bitcast(I32)),
                waits=[t, qh_toks.get(hc), c.s_tok], inc="dv")
            c.l1q_f32_free = t
            t = b.op("vector", lambda e, hc=hc: e.tensor_scalar(
                out=c.a_h[:, hc * 256:(hc + 1) * 256],
                in0=c.a_h[:, hc * 256:(hc + 1) * 256], scalar1=127.0,
                scalar2=-127.0, op0=ALU.min, op1=ALU.max), waits=[t], inc="dv")
            qh_toks[hc] = t
        for dc2 in range(DC):
            ai = dc2 % 2
            ps = c.ps[ai]
            psd = ps[:, 1024:1280]
            mmd = None
            for hc in range(HC):
                slot = (dc2 * HC + hc) % 6
                so = slot * 512 + 256
                dd = b.op("sync", lambda e, hc=hc, dc2=dc2, so=so: e.dma_start(
                    out=c.wring[:, so:so + 128],
                    in_=c.qw2g[dc2 * 128:(dc2 + 1) * 128,
                               hc * 128:(hc + 1) * 128], transpose=True),
                    waits=[slot_free[slot], g2], inc="wl", n=16)
                w0 = [dd, qh_toks[hc]]
                if hc == 0:
                    w0.append(c.arena_free[ai])
                mmd = b.op("tensor", lambda e, hc=hc, so=so, psd=psd: e.matmul(
                    psd, c.wring[:, so:so + 128], c.a_h[:, hc * 256:(hc + 1) * 256],
                    start=(hc == 0), stop=(hc == HC - 1)), waits=w0, inc="pe")
                slot_free[slot] = mmd
            t = b.op("vector", lambda e, psd=psd, dc2=dc2: e.tensor_scalar(
                out=c.f32t[:, 512:768], in0=psd,
                scalar1=c.sclr[:, 72 + dc2:73 + dc2],
                scalar2=c.bp[:, b2col + dc2:b2col + dc2 + 1],
                op0=ALU.mult, op1=ALU.add),
                waits=[mmd, sxs2_tok[dc2], getattr(c, "l1c_free", None)], inc="dv")
            t = b.op("scalar", lambda e: e.activation(
                out=c.f32t[:, 768:1024], in_=c.f32t[:, 512:768], func=AF.Silu),
                waits=[t], inc="ac")
            t = b.op("vector", lambda e, dc2=dc2, t0=t0: e.tensor_tensor(
                out=c.f32t[:, 512:768], in0=c.f32t[:, 768:1024],
                in1=c.a_b[:, dc2 * TL + t0:dc2 * TL + t0 + 256], op=ALU.mult),
                waits=[t, c.s_tok], inc="dv")
            t = b.op("vector", lambda e, dc2=dc2, t0=t0: e.tensor_tensor(
                out=c.x_sb[:, dc2 * TL + t0:dc2 * TL + t0 + 256],
                in0=c.f32t[:, 512:768],
                in1=c.x_sb[:, dc2 * TL + t0:dc2 * TL + t0 + 256], op=ALU.add),
                inc="dv")
            c.l1c_free = t
            c.arena_free[ai] = t
            if tb4 == 3:
                c.combine_toks[dc2] = t
                c.combine_last = t
            last = t
        qh_toks = {hc: last for hc in range(HC)}
    c.combine_last = last




def bin_rows(c, wsrc_dram, rows_n, cols_n, qdst, waits=()):
    b = c.b
    nq = cols_n // 512
    last = None
    for r0 in range(0, rows_n, 128):
        acc_t = None
        for qf in range(nq):
            t = b.op("sync", lambda e, r0=r0, qf=qf: e.dma_start(
                out=c.f32t[:, 0:512],
                in_=wsrc_dram[r0:r0 + 128, qf * 512:(qf + 1) * 512]),
                waits=list(waits) + [getattr(c, "q_f32t_free", None)],
                inc="wl", n=16)
            t = b.op("vector", lambda e: e.tensor_reduce(
                out=c.stat[:, 0:1], in_=c.f32t[:, 0:512], axis=AXX, op=ALU.add,
                apply_absolute_value=True), waits=[t], inc="dv")
            if qf == 0:
                t = b.op("vector", lambda e: e.tensor_copy(
                    out=c.stat[:, 1:2], in_=c.stat[:, 0:1]), waits=[t], inc="dv")
            else:
                t = b.op("vector", lambda e: e.tensor_tensor(
                    out=c.stat[:, 1:2], in0=c.stat[:, 1:2], in1=c.stat[:, 0:1],
                    op=ALU.add), waits=[t], inc="dv")
            c.q_f32t_free = t
            acc_t = t
        alpha_t = b.op("vector", lambda e, cols_n=cols_n: e.tensor_scalar(
            out=c.stat[:, 2:3], in0=c.stat[:, 1:2], scalar1=1.0 / cols_n,
            scalar2=None, op0=ALU.mult), waits=[acc_t], inc="dv")
        for qf in range(nq):
            t = b.op("sync", lambda e, r0=r0, qf=qf: e.dma_start(
                out=c.f32t[:, 0:512],
                in_=wsrc_dram[r0:r0 + 128, qf * 512:(qf + 1) * 512]),
                waits=[getattr(c, "q_f32t_free", None)], inc="wl", n=16)
            t = b.op("scalar", lambda e: e.activation(
                out=c.f32t[:, 512:1024], in_=c.f32t[:, 0:512], func=AF.Sign),
                waits=[t], inc="ac")
            c.q_f32t_free = t
            t = b.op("vector", lambda e: e.tensor_scalar(
                out=c.pp[:, 0:512], in0=c.f32t[:, 512:1024],
                scalar1=c.stat[:, 2:3], scalar2=None, op0=ALU.mult),
                waits=[t, alpha_t, getattr(c, "q_pp_free", None)], inc="dv")
            last = b.op("sync", lambda e, r0=r0, qf=qf: e.dma_start(
                out=qdst[r0:r0 + 128, qf * 512:(qf + 1) * 512],
                in_=c.pp[:, 0:512]), waits=[t], inc="st", n=16)
            c.q_pp_free = last
    return last


def emit_layer2(c):
    b = c.b
    cl_q = cast_weights(c, c.w2q, c.s2q)
    cl_k = cast_weights(c, c.w2k, c.s2k)
    cl_v = cast_weights(c, c.w2v, c.s2v)
    cl_o = cast_weights(c, c.w2o, c.s2o)
    bb = bin_rows(c, c.b_l2w, 256, 2048, c.bwl)
    all8 = [[0, 1, 2, 3, 4, 5, 6, 7]]
    ag = [[0, 1], [2, 3], [4, 5], [6, 7]]
    gb = b.op("gpsimd", lambda e: e.collective_compute(
        "AllGather", ALU.bypass, replica_groups=all8,
        ins=[c.bwl[:, :].opt()], outs=[c.bwg[:, :].opt()]),
        waits=[bb], inc="cc")

    ln(c)
    dbg_dump(c, "xn2", c.a_a, waits=[c.xn_tok])

    def wsrc_nat_scr(scr):
        def f(e, d, ocp, sl):
            return e.dma_start(out=sl[:, 0:256],
                               in_=scr[d * 128:(d + 1) * 128,
                                       ocp * 128:ocp * 128 + 256])
        return f

    stg_free = {0: bb, 1: bb}

    def mk_out(dst, bias_key):
        bcol = BPC[bias_key]

        def f_qk(oc, tb, ps, mmtok):
            hh = oc % 2
            sg = c.pp[:, hh * 1024 + tb * 512:hh * 1024 + (tb + 1) * 512]
            t = b.op("vector", lambda e, ps=ps, sg=sg, oc=oc: e.tensor_scalar(
                out=sg, in0=ps, scalar1=c.bp[:, bcol + oc:bcol + oc + 1],
                scalar2=None, op0=ALU.add),
                waits=[mmtok, stg_free.get(hh)], inc="dv")
            if tb == 1:
                sgf = c.pp[:, hh * 1024:hh * 1024 + 1024]
                if dst == "q":
                    st = b.op("sync", lambda e, oc=oc, sgf=sgf: e.dma_start(
                        out=c.qb[oc, :, :], in_=sgf), waits=[t], inc="st", n=16)
                else:
                    st = b.op("sync", lambda e, oc=oc, sgf=sgf: e.dma_start(
                        out=c.kl[oc * 128:(oc + 1) * 128, :], in_=sgf),
                        waits=[t], inc="st", n=16)
                stg_free[hh] = st
            return t

        def f_v(oc, tb, ps, mmtok):
            return b.op("vector", lambda e, ps=ps, oc=oc, tb=tb: e.tensor_scalar(
                out=c.a_h[:, oc * TL + tb * 512:oc * TL + (tb + 1) * 512],
                in0=ps, scalar1=c.bp[:, bcol + oc:bcol + oc + 1],
                scalar2=None, op0=ALU.add),
                waits=[mmtok, c.ln_ah_free], inc="dv")
        return f_v if dst == "v" else f_qk

    proj(c, wsrc=wsrc_nat_scr(c.s2q), O=2048,
         rhs=lambda d, tb: c.a_a[:, d * TL + tb * 512:d * TL + (tb + 1) * 512],
         out_cb=mk_out("q", "l2_bq"), rhs_tok=c.xn_tok, extra_wait=cast_tok(cl_q))
    proj(c, wsrc=wsrc_nat_scr(c.s2k), O=2048,
         rhs=lambda d, tb: c.a_a[:, d * TL + tb * 512:d * TL + (tb + 1) * 512],
         out_cb=mk_out("k", "l2_bk"), rhs_tok=c.xn_tok, extra_wait=cast_tok(cl_k))
    proj(c, wsrc=wsrc_nat_scr(c.s2v), O=2048,
         rhs=lambda d, tb: c.a_a[:, d * TL + tb * 512:d * TL + (tb + 1) * 512],
         out_cb=mk_out("v", "l2_bv"), rhs_tok=c.xn_tok, extra_wait=cast_tok(cl_v))
    vfm_tok = b.tok("dv")
    emit_v_transpose(c, vfm_tok, v_chunks=16)
    kv_written = b.tok("st")
    kg_tok = b.op("gpsimd", lambda e: e.collective_compute(
        "AllGather", ALU.bypass, replica_groups=ag,
        ins=[c.kl[:, :].opt()], outs=[c.kg[:, :].opt()]),
        waits=[kv_written], inc="cc")
    vg_tok = b.op("gpsimd", lambda e: e.collective_compute(
        "AllGather", ALU.bypass, replica_groups=ag,
        ins=[c.vl[:, :].opt()], outs=[c.vg[:, :].opt()]),
        waits=[kv_written], inc="cc")

    def ffn_out(oc, tb, ps, mmtok):
        bcol = BPC["l2_bias"]
        t = b.op("vector", lambda e, ps=ps, oc=oc: e.tensor_scalar(
            out=c.f32t[:, 0:512], in0=ps,
            scalar1=c.bp[:, bcol + oc:bcol + oc + 1], scalar2=None,
            op0=ALU.add), waits=[mmtok, getattr(c, "l2f_free", None)], inc="dv")
        t = b.op("scalar", lambda e: e.activation(
            out=c.f32t[:, 512:1024], in_=c.f32t[:, 0:512], func=AF.Silu),
            waits=[t, getattr(c, "l2g_free", None)], inc="ac")
        c.l2f_free = t
        t = b.op("vector", lambda e, oc=oc, tb=tb: e.tensor_copy(
            out=c.a_b[:, oc * TL + tb * 512:oc * TL + (tb + 1) * 512],
            in_=c.f32t[:, 512:1024]), waits=[t], inc="dv")
        c.l2g_free = t
        return t

    def wsrc_tr2(e, d, ocp, sl):
        return e.dma_start(out=sl[:, 0:256],
                           in_=c.bwg[ocp * 128:ocp * 128 + 256,
                                     d * 128:(d + 1) * 128], transpose=True)

    proj(c, wsrc=wsrc_tr2, O=2048,
         rhs=lambda d, tb: c.a_a[:, d * TL + tb * 512:d * TL + (tb + 1) * 512],
         out_cb=ffn_out, rhs_tok=c.xn_tok, extra_wait=gb)
    c.g_tok = b.tok("dv")

    emit_attn_l2(c, kg_tok, vg_tok, cl_o)


def emit_attn_l2(c, kg_tok, vg_tok, cl_o):
    b = c.b
    psA, psB = c.psA, c.psB
    SC = float(1.0 / np.sqrt(32.0))
    kh_free = [None, None]
    vh_free = [None, None]
    wk_ready = [b.tok("pe")]
    io = {}

    def load_head(hp):
        sl = hp % 2
        kt = vt = None
        for g in range(16):
            rank, loc = g & 1, g >> 1
            kt = b.op("sync", lambda e, hp=hp, sl=sl, g=g, rank=rank, loc=loc: e.dma_start(
                out=c.kh[:, sl * 2048 + g * 128:sl * 2048 + (g + 1) * 128],
                in_=c.kg[rank * 2048 + hp * 128:rank * 2048 + (hp + 1) * 128,
                         loc * 128:(loc + 1) * 128]),
                waits=[kg_tok, kh_free[sl]] + wk_ready, inc="wl", n=16)
            vt = b.op("sync", lambda e, hp=hp, sl=sl, g=g, rank=rank, loc=loc: e.dma_start(
                out=c.vh[:, sl * 2080 + g * 128:sl * 2080 + g * 128 + 128],
                in_=c.vg[rank * 1024 + loc * 128:rank * 1024 + (loc + 1) * 128,
                         hp * 128:(hp + 1) * 128]),
                waits=[vg_tok, vh_free[sl]], inc="wl", n=16)
        return kt, vt

    def load_q(hp, qf):
        return b.op("sync", lambda e, hp=hp: e.dma_start(
            out=c.qh[:, :], in_=c.qb[hp, :, :]), waits=[qf], inc="wl", n=16)

    io[0] = load_head(0)
    io[1] = load_head(1)
    qt = load_q(0, None)
    sslot_free = [c.arena_free[0]] * 3
    cexp_prev = [None, None]
    pp_free = [None] * 4
    av_free = c.arena_free[1]
    den_free = [c.arena_free[1], c.arena_free[1]]
    rb_free = c.arena_free[1]
    mk_free = [None, None]
    ah_w = getattr(c, "vtr_done", None)
    last_smm = None
    NHP = 16
    for hp in range(NHP):
        sl = hp % 2
        kt, vt = io[hp]
        qt_next = None
        for tb in range(2):
            av_mms = [None, None]
            den_mms = [None, None]
            for idx in range(16):
                kk = idx
                for a in range(2):
                    ss = (hp * 32 + tb * 16 + idx * 2 + a) % 3
                    smm = b.op("tensor", lambda e, sl=sl, tb=tb, kk=kk, ss=ss, a=a: e.matmul(
                        psA[:, ss * 512:(ss + 1) * 512],
                        c.kh[a * 64:(a + 1) * 64,
                             sl * 2048 + kk * 128:sl * 2048 + (kk + 1) * 128],
                        c.qh[a * 64:(a + 1) * 64, tb * 512:(tb + 1) * 512],
                        start=True, stop=True),
                        waits=[kt, qt, sslot_free[ss]], inc="pe")
                    last_smm = smm
                    if idx == 0:
                        pc = a
                        t = b.op("vector", lambda e, ss=ss, pc=pc: e.tensor_reduce(
                            out=c.stat[:, 8 + pc:9 + pc],
                            in_=psA[:, ss * 512:(ss + 1) * 512],
                            axis=AXX, op=ALU.max), waits=[smm], inc="dv")
                        t = rowmax_to_scalar(c, 8 + pc, 2060 + pc, [t])
                        cbt = bcast_scalar(c, 2060 + pc, 5 + pc, [t], scale=-SC,
                                           extra=[cexp_prev[pc]])
                        if a == 0:
                            cb0 = cbt
                        else:
                            cb1 = cbt
                    cb = cb0 if a == 0 else cb1
                    pslot = (idx * 2 + a) % 4
                    texp = b.op("scalar", lambda e, ss=ss, pslot=pslot, a=a: e.activation(
                        out=c.pp[:, pslot * 512:(pslot + 1) * 512],
                        in_=psA[:, ss * 512:(ss + 1) * 512], func=AF.Exp,
                        scale=SC, bias=c.sclr[:, 5 + a:6 + a]),
                        waits=[smm, cb, pp_free[pslot]], inc="ac")
                    cexp_prev[a] = texp
                    sslot_free[ss] = texp
                    den_mms[a] = b.op("tensor", lambda e, pslot=pslot, idx=idx, a=a: e.matmul(
                        psB[0:1, 512 + a * 512:1024 + a * 512],
                        c.ones_c[:, 0:1],
                        c.pp[:, pslot * 512:(pslot + 1) * 512],
                        start=(idx == 0), stop=(idx == 15)),
                        waits=[texp, c.ones_tok] + ([den_free[a]] if idx == 0 else []),
                        inc="pe")
                    av_mms[a] = b.op("tensor", lambda e, pslot=pslot, sl=sl, kk=kk, idx=idx, a=a: e.matmul(
                        psB[a * 64:(a + 1) * 64, 0:512],
                        c.vh[:, sl * 2080 + kk * 128 + a * 64:sl * 2080 + kk * 128 + a * 64 + 64],
                        c.pp[:, pslot * 512:(pslot + 1) * 512],
                        start=(idx == 0), stop=(idx == 15)),
                        waits=[texp, vt] + ([av_free] if idx == 0 else []), inc="pe")
                    pp_free[pslot] = av_mms[a]
            t0 = b.op("vector", lambda e: e.reciprocal(
                out=c.rows[0:1, 1024:1536], in_=psB[0:1, 512:1024]),
                waits=[den_mms[0], rb_free], inc="dv")
            t1 = b.op("vector", lambda e: e.reciprocal(
                out=c.rows[0:1, 1536:2048], in_=psB[0:1, 1024:1536]),
                waits=[den_mms[1]], inc="dv")
            den_free = [t0, t1]
            tb0 = b.op("tensor", lambda e: e.matmul(
                psB[0:64, 1536:2048], c.ones_rf[0:1, 0:64],
                c.rows[0:1, 1024:1536], start=True, stop=True),
                waits=[t0, rb_free], inc="pe")
            tb1 = b.op("tensor", lambda e: e.matmul(
                psB[64:128, 1536:2048], c.ones_rf[0:1, 0:64],
                c.rows[0:1, 1536:2048], start=True, stop=True),
                waits=[t1], inc="pe")
            t = b.op("vector", lambda e: e.tensor_copy(
                out=c.mk[:, 0:512], in_=psB[:, 1536:2048]),
                waits=[tb0, tb1, mk_free[0]], inc="dv")
            rb_free = t
            t = b.op("vector", lambda e, hp=hp, tb=tb: e.tensor_tensor(
                out=c.a_h[:, hp * TL + tb * 512:hp * TL + (tb + 1) * 512],
                in0=psB[:, 0:512], in1=c.mk[:, 0:512], op=ALU.mult),
                waits=[av_mms[0], av_mms[1], t, ah_w], inc="dv")
            av_free = t
            mk_free[0] = t
            if tb == 1:
                qt_next = load_q(hp + 1, last_smm) if hp + 1 < NHP else None
                kh_free[sl] = last_smm
                vh_free[sl] = av_mms[1]
                if hp + 2 < NHP:
                    io[hp + 2] = load_head(hp + 2)
        qt = qt_next
    c.l2_att_done = b.tok("dv")
    c.arena_free = [b.tok("ac"), c.l2_att_done]
    c.wring_free = [b.tok("pe")] * 6
    c.pt_free = None

    bocol = BPC["l2_bo"]

    def wo_out(oc, tb, ps, mmtok):
        b.op("vector", lambda e, ps=ps, oc=oc: e.tensor_scalar(
            out=c.f32t[:, 0:512], in0=ps,
            scalar1=c.bp[:, bocol + oc:bocol + oc + 1], scalar2=None,
            op0=ALU.add), waits=[mmtok, c.g_tok,
                                 getattr(c, "l2c_free", None)], inc="dv")
        b.op("vector", lambda e, oc=oc, tb=tb: e.tensor_tensor(
            out=c.f32t[:, 512:1024], in0=c.f32t[:, 0:512],
            in1=c.a_b[:, oc * TL + tb * 512:oc * TL + (tb + 1) * 512],
            op=ALU.mult), inc="dv")
        t = b.op("vector", lambda e, oc=oc, tb=tb: e.tensor_tensor(
            out=c.x_sb[:, oc * TL + tb * 512:oc * TL + (tb + 1) * 512],
            in0=c.f32t[:, 512:1024],
            in1=c.x_sb[:, oc * TL + tb * 512:oc * TL + (tb + 1) * 512],
            op=ALU.add), inc="dv")
        c.l2c_free = t
        if tb == 1:
            c.combine_toks[oc] = t
            c.combine_last = t
        return t

    def wsrc_nat_o(e, d, ocp, sl):
        return e.dma_start(out=sl[:, 0:256],
                           in_=c.s2o[d * 128:(d + 1) * 128,
                                     ocp * 128:ocp * 128 + 256])

    proj(c, wsrc=wsrc_nat_o, O=2048,
         rhs=lambda d, tb: c.a_h[:, d * TL + tb * 512:d * TL + (tb + 1) * 512],
         out_cb=wo_out, rhs_tok=c.l2_att_done, extra_wait=cast_tok(cl_o))




def ghost_build(c, idx_dram, rows_n, qdst, lut_tok, waits=()):
    b = c.b
    cols_n = idx_dram.shape[1]
    last = None
    for r0 in range(0, rows_n, 128):
        rh = min(128, rows_n - r0)
        for q0 in range(0, cols_n, 512):
            qw = min(512, cols_n - q0)
            t = b.op("sync", lambda e, r0=r0, rh=rh, q0=q0, qw=qw: e.dma_start(
                out=c.f32t[:, 0:512].bitcast(I32)[0:rh, 0:qw],
                in_=idx_dram[r0:r0 + rh, q0:q0 + qw]),
                waits=list(waits) + [getattr(c, "gb_f32_free", None)],
                inc="wl", n=16)
            t = b.op("vector", lambda e, rh=rh, qw=qw: e.tensor_copy(
                out=c.pp[0:rh, 0:qw],
                in_=c.f32t[:, 0:512].bitcast(I32)[0:rh, 0:qw]),
                waits=[t, getattr(c, "gb_pp_free", None)], inc="dv")
            c.gb_f32_free = t
            acc0 = b.op("vector", lambda e, rh=rh, qw=qw: e.tensor_scalar(
                out=c.pp[0:rh, 512:512 + qw], in0=c.pp[0:rh, 0:qw],
                scalar1=0.0, scalar2=c.lutb[0:rh, 0:1], op0=ALU.is_equal,
                op1=ALU.mult),
                waits=[t, lut_tok, getattr(c, "gb_pp_free", None)], inc="dv")
            for j in range(1, 16):
                tj = b.op("vector", lambda e, rh=rh, qw=qw, j=j: e.tensor_scalar(
                    out=c.pp[0:rh, 1024:1024 + qw], in0=c.pp[0:rh, 0:qw],
                    scalar1=float(j), scalar2=c.lutb[0:rh, j:j + 1],
                    op0=ALU.is_equal, op1=ALU.mult), inc="dv")
                acc0 = b.op("vector", lambda e, rh=rh, qw=qw: e.tensor_tensor(
                    out=c.pp[0:rh, 512:512 + qw], in0=c.pp[0:rh, 1024:1024 + qw],
                    in1=c.pp[0:rh, 512:512 + qw], op=ALU.add),
                    waits=[tj], inc="dv")
            last = b.op("sync", lambda e, r0=r0, rh=rh, q0=q0, qw=qw: e.dma_start(
                out=qdst[r0:r0 + rh, q0:q0 + qw], in_=c.pp[0:rh, 512:512 + qw]),
                waits=[acc0], inc="st", n=16)
            c.gb_pp_free = last
    return last


def emit_layer3(c):
    b = c.b
    cl_q = cast_weights(c, c.w3q, c.s3q)
    cl_k = cast_weights(c, c.w3k, c.s3k)
    cl_v = cast_weights(c, c.w3v, c.s3v)
    cl_o = cast_weights(c, c.w3o, c.s3o)
    lut_tok = b.op("sync", lambda e: e.dma_start(
        out=c.lutb[:, :], in_=c.lut_d[:, :]), inc="wl", n=16)
    rt = b.op("sync", lambda e: e.dma_start(out=c.rotb[:, :], in_=c.rot_d[:, :]),
              inc="wl", n=16)
    def snap_lut(k, w):
        if "lt32" in c.dbg:
            b.op("sync", lambda e, k=k: e.dma_start(
                out=c.dbg["lt32"][k, :, 0:16], in_=c.lutb[:, :]),
                waits=[w], inc="st", n=16)

    snap_lut(0, lut_tok)
    g1t = ghost_build(c, c.b_gidx, 688, c.gwl, lut_tok)
    snap_lut(1, g1t)
    g2t = ghost_build(c, c.b_uidx, 688, c.uwl, lut_tok, waits=[g1t])
    snap_lut(2, g2t)
    g3t = ghost_build(c, c.b_didx, 256, c.dwl, lut_tok, waits=[g2t])
    snap_lut(3, g3t)
    all8 = [[0, 1, 2, 3, 4, 5, 6, 7]]
    ag = [[0, 1], [2, 3], [4, 5], [6, 7]]
    gg = b.op("gpsimd", lambda e: e.collective_compute(
        "AllGather", ALU.bypass, replica_groups=all8,
        ins=[c.gwl[:, :].opt()], outs=[c.gwg[:, :].opt()]),
        waits=[g3t], inc="cc")
    gu = b.op("gpsimd", lambda e: e.collective_compute(
        "AllGather", ALU.bypass, replica_groups=all8,
        ins=[c.uwl[:, :].opt()], outs=[c.uwg[:, :].opt()]),
        waits=[g3t], inc="cc")
    gd = b.op("gpsimd", lambda e: e.collective_compute(
        "AllGather", ALU.bypass, replica_groups=all8,
        ins=[c.dwl[:, :].opt()], outs=[c.dwg[:, :].opt()]),
        waits=[g3t], inc="cc")

    if "gw3" in c.dbg:
        for ch in range(DC):
            b.op("sync", lambda e, ch=ch: e.dma_start(
                out=c.dbg["gw3"][ch, :, :],
                in_=c.gwg[ch * 128:(ch + 1) * 128, 0:TL]),
                waits=[gg], inc="st", n=16)
    if "uw3" in c.dbg:
        for ch in range(DC):
            b.op("sync", lambda e, ch=ch: e.dma_start(
                out=c.dbg["uw3"][ch, :, :],
                in_=c.uwg[ch * 128:(ch + 1) * 128, 0:TL]),
                waits=[gu], inc="st", n=16)
    if "dw3" in c.dbg:
        for ch in range(DC):
            b.op("sync", lambda e, ch=ch: e.dma_start(
                out=c.dbg["dw3"][ch, :, :],
                in_=c.dwg[ch * 128:(ch + 1) * 128, 0:TL]),
                waits=[gd], inc="st", n=16)
    ln(c)
    dbg_dump(c, "xn3", c.a_a, waits=[c.xn_tok])
    ct = b.op("sync", lambda e: e.dma_start(out=c.cosb[:, :], in_=c.cos_d[:, :]),
              waits=[getattr(c, "gb_pp_free", None)], inc="wl", n=16)
    ct = b.op("sync", lambda e: e.dma_start(out=c.sinb[:, :], in_=c.sin_d[:, :]),
              inc="wl", n=16)

    def wsrc_tr(scr):
        def f(e, d, ocp, sl):
            return e.dma_start(out=sl[:, 0:256],
                               in_=scr[ocp * 128:ocp * 128 + 256,
                                       d * 128:(d + 1) * 128], transpose=True)
        return f

    rope_state = {}

    def mk_rope_out(dst):
        def f(oc, tb, ps, mmtok):
            scr = c.ps[(oc // 2) % 2][:, 0:512]
            t = b.op("vector", lambda e, ps=ps: e.tensor_copy(
                out=c.pt[:, 0:512], in_=ps),
                waits=[mmtok, rope_state.get("pt0")], inc="dv")
            tr = b.op("tensor", lambda e, scr=scr: e.matmul(
                scr, c.rotb[:, :], c.pt[:, 0:512],
                start=True, stop=True),
                waits=[t, rt, getattr(c, "mb_free", None)], inc="pe")
            t2 = b.op("vector", lambda e, scr=scr: e.tensor_copy(
                out=c.pt[:, 512:1024], in_=scr),
                waits=[tr, rope_state.get("pt1")], inc="dv")
            c.mb_free = t2
            t3 = b.op("vector", lambda e, tb=tb: e.tensor_tensor(
                out=c.pt[:, 1024:1536], in0=c.pt[:, 0:512],
                in1=c.cosb[:, tb * 512:(tb + 1) * 512], op=ALU.mult),
                waits=[t2, ct, rope_state.get("pt2")], inc="dv")
            rope_state["pt0"] = t3
            t4 = b.op("vector", lambda e, tb=tb: e.tensor_tensor(
                out=c.pt[:, 1536:2048], in0=c.pt[:, 512:1024],
                in1=c.sinb[:, tb * 512:(tb + 1) * 512], op=ALU.mult),
                waits=[t3, rope_state.get("pt3")], inc="dv")
            rope_state["pt1"] = t4
            t5 = b.op("vector", lambda e: e.tensor_tensor(
                out=c.pt[:, 1024:1536], in0=c.pt[:, 1024:1536],
                in1=c.pt[:, 1536:2048], op=ALU.add), waits=[t4], inc="dv")
            rope_state["pt3"] = t5
            if dst == "q":
                st = b.op("sync", lambda e, oc=oc, tb=tb: e.dma_start(
                    out=c.qb[oc, :, tb * 512:(tb + 1) * 512],
                    in_=c.pt[:, 1024:1536]), waits=[t5], inc="st", n=16)
            else:
                st = b.op("sync", lambda e, oc=oc, tb=tb: e.dma_start(
                    out=c.k3l[oc * 128:(oc + 1) * 128, tb * 512:(tb + 1) * 512],
                    in_=c.pt[:, 1024:1536]), waits=[t5], inc="st", n=16)
            rope_state["pt2"] = st
            return st
        return f

    def v_out(oc, tb, ps, mmtok):
        return b.op("vector", lambda e, ps=ps, oc=oc, tb=tb: e.tensor_copy(
            out=c.a_h[:, oc * TL + tb * 512:oc * TL + (tb + 1) * 512], in_=ps),
            waits=[mmtok, c.ln_ah_free], inc="dv")

    proj(c, wsrc=wsrc_tr(c.s3q), O=2048,
         rhs=lambda d, tb: c.a_a[:, d * TL + tb * 512:d * TL + (tb + 1) * 512],
         out_cb=mk_rope_out("q"), rhs_tok=c.xn_tok, extra_wait=cast_tok(cl_q))
    proj(c, wsrc=wsrc_tr(c.s3k), O=512,
         rhs=lambda d, tb: c.a_a[:, d * TL + tb * 512:d * TL + (tb + 1) * 512],
         out_cb=mk_rope_out("k"), rhs_tok=c.xn_tok, extra_wait=cast_tok(cl_k))
    proj(c, wsrc=wsrc_tr(c.s3v), O=512,
         rhs=lambda d, tb: c.a_a[:, d * TL + tb * 512:d * TL + (tb + 1) * 512],
         out_cb=v_out, rhs_tok=c.xn_tok, extra_wait=cast_tok(cl_v))
    vfm_tok = b.tok("dv")
    emit_v3_transpose(c, vfm_tok)
    kv_written = b.tok("st")
    kg_tok = b.op("gpsimd", lambda e: e.collective_compute(
        "AllGather", ALU.bypass, replica_groups=ag,
        ins=[c.k3l[:, :].opt()], outs=[c.k3g[:, :].opt()]),
        waits=[kv_written], inc="cc")
    vg_tok = b.op("gpsimd", lambda e: e.collective_compute(
        "AllGather", ALU.bypass, replica_groups=ag,
        ins=[c.v3l[:, :].opt()], outs=[c.v3g[:, :].opt()]),
        waits=[kv_written], inc="cc")
    if "q3" in c.dbg:
        for ch in range(16):
            b.op("sync", lambda e, ch=ch: e.dma_start(
                out=c.dbg["q3"][ch, :, :], in_=c.qb[ch, :, :]),
                waits=[kv_written], inc="st", n=16)
    if "k3" in c.dbg:
        for ch in range(8):
            b.op("sync", lambda e, ch=ch: e.dma_start(
                out=c.dbg["k3"][ch, :, :], in_=c.k3g[ch * 128:(ch + 1) * 128, :]),
                waits=[kg_tok], inc="st", n=16)
    if "v3" in c.dbg:
        for ch in range(16):
            b.op("sync", lambda e, ch=ch: e.dma_start(
                out=c.dbg["v3"][ch, :, 0:512],
                in_=c.v3g[ch * 128:(ch + 1) * 128, :]),
                waits=[vg_tok], inc="st", n=16)

    emit_ffn_swiglu(c, scr_g=c.gwg, scr_u=c.uwg, scr_d=c.dwg, H=5504,
                    cast_g=[(5504, gg)], cast_u=[(5504, gu)], cast_d=[(2048, gd)],
                    rhs_arena=c.a_a, rhs_tok=c.xn_tok,
                    vtr_done=getattr(c, "vtr_done", None))
    dbg_dump(c, "g3", c.a_b, waits=[c.g_tok])

    emit_attn_l3(c, kg_tok, vg_tok, cl_o)


def emit_v3_transpose(c, vfm_tok):
    b = c.b
    psB = c.psB
    slots = [1024, 1536]
    slot_free = [c.arena_free[1], c.arena_free[1]]
    st_free = {}
    for vv in range(4):
        for jt in range(NT):
            si = (vv * NT + jt) % 2
            off = slots[si]
            fsl = c.f32t[:, si * 128:(si + 1) * 128]
            tf = b.op("vector", lambda e, vv=vv, jt=jt, fsl=fsl: e.tensor_copy(
                out=fsl, in_=c.a_h[:, vv * TL + jt * 128:vv * TL + (jt + 1) * 128]),
                waits=[vfm_tok, st_free.get(("f", si))], inc="dv")
            tt = b.op("tensor", lambda e, off=off, fsl=fsl: e.transpose(
                psB[:, off:off + 128], fsl, c.identf[:, :]),
                waits=[tf, slot_free[si], c.const_tok], inc="pe")
            st_free[("f", si)] = tt
            tv = b.op("vector", lambda e, off=off, si=si: e.tensor_copy(
                out=c.pt[:, si * 128:(si + 1) * 128], in_=psB[:, off:off + 128]),
                waits=[tt, st_free.get(("s", si))], inc="dv")
            slot_free[si] = tv
            stv = b.op("sync", lambda e, jt=jt, vv=vv, si=si: e.dma_start(
                out=c.v3l[jt * 128:(jt + 1) * 128, vv * 128:(vv + 1) * 128],
                in_=c.pt[:, si * 128:(si + 1) * 128]), waits=[tv], inc="st", n=16)
            st_free[("s", si)] = stv
    c.vtr_done = b.tok("st")
    c.arena_free[1] = b.tok("dv")
    c.pt_free = b.tok("st")


def emit_attn_l3(c, kg_tok, vg_tok, cl_o):
    b = c.b
    psA, psB = c.psA, c.psB
    SC = float(1.0 / np.sqrt(128.0))
    kh_free = [None, None]
    vh_free = [None, None]
    wk_ready = [b.tok("pe")]
    io = {}

    def load_kv(g2):
        sl = g2 % 2
        kt = vt = None
        for g in range(16):
            rank, loc = g & 1, g >> 1
            kt = b.op("sync", lambda e, g2=g2, sl=sl, g=g, rank=rank, loc=loc: e.dma_start(
                out=c.kh[:, sl * 2048 + g * 128:sl * 2048 + (g + 1) * 128],
                in_=c.k3g[rank * 512 + g2 * 128:rank * 512 + (g2 + 1) * 128,
                          loc * 128:(loc + 1) * 128]),
                waits=[kg_tok, kh_free[sl]] + wk_ready, inc="wl", n=16)
            vt = b.op("sync", lambda e, g2=g2, sl=sl, g=g, rank=rank, loc=loc: e.dma_start(
                out=c.vh[:, sl * 2080 + g * 128:sl * 2080 + g * 128 + 128],
                in_=c.v3g[rank * 1024 + loc * 128:rank * 1024 + (loc + 1) * 128,
                          g2 * 128:(g2 + 1) * 128]),
                waits=[vg_tok, vh_free[sl]], inc="wl", n=16)
        return kt, vt

    def load_q(qh_i, qf):
        return b.op("sync", lambda e, qh_i=qh_i: e.dma_start(
            out=c.qh[:, :], in_=c.qb[qh_i, :, :]), waits=[qf], inc="wl", n=16)

    io[0] = load_kv(0)
    io[1] = load_kv(1)
    qt = load_q(0, None)
    sslot_free = [c.arena_free[0]] * 3
    cexp_prev = [None, None]
    pp_free = [None] * 4
    mk_free = [None, None]
    av_free = c.arena_free[1]
    den_free = c.arena_free[1]
    rb_free = c.arena_free[1]
    ah_w = getattr(c, "vtr_done", None)
    last_smm = None
    m3_off = [0]
    for t in range(1, 4):
        m3_off.append(m3_off[-1] + len(L3_CHUNKLIST[t - 1]))
    scount = 0
    for g2 in range(4):
        sl = g2 % 2
        kt, vt = io[g2]
        for qa in range(4):
            qh_i = 4 * g2 + qa
            qt_next = None
            for tb in range(4):
                ks = L3_CHUNKLIST[tb]
                order = [ks[-1]] + ks[:-1]
                nk = len(order)
                av_mm = None
                den_mm = None
                for idx, kk in enumerate(order):
                    ss = scount % 3
                    scount += 1
                    smm = b.op("tensor", lambda e, sl=sl, tb=tb, kk=kk, ss=ss: e.matmul(
                        psA[:, ss * 512:ss * 512 + 256],
                        c.kh[:, sl * 2048 + kk * 128:sl * 2048 + (kk + 1) * 128],
                        c.qh[:, tb * 256:(tb + 1) * 256], start=True, stop=True),
                        waits=[kt, qt, sslot_free[ss]], inc="pe")
                    last_smm = smm
                    if idx == 0:
                        pc = (qh_i * 4 + tb) % 2
                        t = b.op("vector", lambda e, ss=ss, pc=pc: e.tensor_reduce(
                            out=c.stat[:, 8 + pc:9 + pc],
                            in_=psA[:, ss * 512:ss * 512 + 256],
                            axis=AXX, op=ALU.max), waits=[smm], inc="dv")
                        t = rowmax_to_scalar(c, 8 + pc, 2060 + pc, [t])
                        cb = bcast_scalar(c, 2060 + pc, 5 + pc, [t], scale=-SC,
                                          extra=[cexp_prev[pc]])
                    pslot = idx % 4
                    texp = b.op("scalar", lambda e, ss=ss, pslot=pslot, pc=pc: e.activation(
                        out=c.pp[:, pslot * 512:pslot * 512 + 256],
                        in_=psA[:, ss * 512:ss * 512 + 256], func=AF.Exp,
                        scale=SC, bias=c.sclr[:, 5 + pc:6 + pc]),
                        waits=[smm, cb, pp_free[pslot]], inc="ac")
                    cexp_prev[pc] = texp
                    sslot_free[ss] = texp
                    midx = m3_off[tb] + ks.index(kk)
                    ms = idx % 2
                    mt = b.op("sync", lambda e, midx=midx, ms=ms: e.dma_start(
                        out=c.mk[:, ms * 512:ms * 512 + 256],
                        in_=c.m3_d[midx, :, :]),
                        waits=[mk_free[ms]], inc="wl", n=16)
                    ptok = b.op("vector", lambda e, pslot=pslot, ms=ms: e.tensor_tensor(
                        out=c.pp[:, pslot * 512:pslot * 512 + 256],
                        in0=c.pp[:, pslot * 512:pslot * 512 + 256],
                        in1=c.mk[:, ms * 512:ms * 512 + 256], op=ALU.mult),
                        waits=[texp, mt], inc="dv")
                    mk_free[ms] = ptok
                    den_mm = b.op("tensor", lambda e, pslot=pslot, idx=idx, nk=nk: e.matmul(
                        psB[0:1, 512:768], c.ones_c[:, 0:1],
                        c.pp[:, pslot * 512:pslot * 512 + 256],
                        start=(idx == 0), stop=(idx == nk - 1)),
                        waits=[ptok, c.ones_tok] + ([den_free] if idx == 0 else []),
                        inc="pe")
                    av_mm = b.op("tensor", lambda e, pslot=pslot, sl=sl, kk=kk, idx=idx, nk=nk, g2=g2: e.matmul(
                        psB[:, 0:256],
                        c.vh[:, sl * 2080 + kk * 128:sl * 2080 + kk * 128 + 128],
                        c.pp[:, pslot * 512:pslot * 512 + 256],
                        start=(idx == 0), stop=(idx == nk - 1)),
                        waits=[ptok, vt] + ([av_free] if idx == 0 else []), inc="pe")
                    pp_free[pslot] = av_mm
                t = b.op("vector", lambda e: e.reciprocal(
                    out=c.rows[0:1, 1024:1280], in_=psB[0:1, 512:768]),
                    waits=[den_mm, rb_free], inc="dv")
                den_free = t
                t = b.op("tensor", lambda e: e.matmul(
                    psB[:, 1536:1792], c.ones_rf[0:1, :], c.rows[0:1, 1024:1280],
                    start=True, stop=True), waits=[t, rb_free], inc="pe")
                t = b.op("vector", lambda e: e.tensor_copy(
                    out=c.mk[:, 0:256], in_=psB[:, 1536:1792]),
                    waits=[t, mk_free[0]], inc="dv")
                rb_free = t
                t = b.op("vector", lambda e, qh_i=qh_i, tb=tb: e.tensor_tensor(
                    out=c.a_h[:, qh_i * TL + tb * 256:qh_i * TL + (tb + 1) * 256],
                    in0=psB[:, 0:256], in1=c.mk[:, 0:256], op=ALU.mult),
                    waits=[av_mm, t, ah_w], inc="dv")
                av_free = t
                mk_free[0] = t
            if qa == 3:
                kh_free[sl] = last_smm
                vh_free[sl] = av_mm
                if g2 + 2 < 4:
                    io[g2 + 2] = load_kv(g2 + 2)
            qt = load_q(qh_i + 1, last_smm) if qh_i + 1 < 16 else None
    c.l3_att_done = b.tok("dv")
    dbg_dump(c, "av3", c.a_h, waits=[c.l3_att_done])
    c.arena_free = [b.tok("ac"), c.l3_att_done]
    c.wring_free = [b.tok("pe")] * 6
    c.pt_free = None

    def wo_out(oc, tb, ps, mmtok):
        b.op("vector", lambda e, ps=ps, oc=oc, tb=tb: e.tensor_tensor(
            out=c.f32t[:, 0:512], in0=ps,
            in1=c.a_b[:, oc * TL + tb * 512:oc * TL + (tb + 1) * 512],
            op=ALU.mult), waits=[mmtok, c.g_tok,
                                 getattr(c, "l3c_free", None)], inc="dv")
        t = b.op("vector", lambda e, oc=oc, tb=tb: e.tensor_tensor(
            out=c.x_sb[:, oc * TL + tb * 512:oc * TL + (tb + 1) * 512],
            in0=c.f32t[:, 0:512],
            in1=c.x_sb[:, oc * TL + tb * 512:oc * TL + (tb + 1) * 512],
            op=ALU.add), inc="dv")
        c.l3c_free = t
        if tb == 1:
            c.combine_toks[oc] = t
            c.combine_last = t
        return t

    def wsrc_tr_o(e, d, ocp, sl):
        return e.dma_start(out=sl[:, 0:256],
                           in_=c.s3o[ocp * 128:ocp * 128 + 256,
                                     d * 128:(d + 1) * 128], transpose=True)

    proj(c, wsrc=wsrc_tr_o, O=2048,
         rhs=lambda d, tb: c.a_h[:, d * TL + tb * 512:d * TL + (tb + 1) * 512],
         out_cb=wo_out, rhs_tok=c.l3_att_done, extra_wait=cast_tok(cl_o))




_prog_cache = {}


def _get_prog(layers):
    key = (tuple(layers), tuple(DEBUG))
    if key not in _prog_cache:
        _prog_cache[key] = build_program(list(layers))
    return _prog_cache[key]


def _core_rows(core):
    p = core % 2
    return np.concatenate(
        [np.arange(128 * (2 * j + p), 128 * (2 * j + p) + 128) for j in range(NT)])


def kernel(**inputs):
    x = np.asarray(inputs["L0_attn_x"], np.float32)
    x_fm = []
    for core in range(NC):
        bidx = core // 2
        x_fm.append(_fm(np.ascontiguousarray(x[bidx][_core_rows(core)])))

    layer_groups = [list(LAYERS)] if FUSED else [[li] for li in LAYERS]
    dbg_results = {}
    for grp in layer_groups:
        nc_prog, _ = _get_prog(tuple(grp))
        maps = make_in_maps(inputs, x_fm, grp)
        res = run_bass_kernel_spmd(nc_prog, maps, core_ids=list(range(NC)))
        x_fm = [np.asarray(r["x_out"]) for r in res.results]
        for name in DEBUG:
            dbg_results[name] = [np.asarray(r["dbg_" + name]) for r in res.results]
    kernel.dbg = dbg_results

    out = np.zeros((4, 2048, 2048), np.float32)
    for core in range(NC):
        bidx = core // 2
        out[bidx][_core_rows(core)] = x_fm[core].transpose(2, 0, 1).reshape(TL, D)
    return out
